# version 15
# baseline (speedup 1.0000x reference)
import numpy as np
import ml_dtypes
import concourse.bass as bass
import concourse.mybir as mybir
from concourse.bass_utils import run_bass_kernel_spmd

F32 = mybir.dt.float32
BF16 = mybir.dt.bfloat16
I32 = mybir.dt.int32
AF = mybir.ActivationFunctionType
ALU = mybir.AluOpType
AX = mybir.AxisListType

ENGS = ("sync", "gpsimd", "vector", "scalar", "tensor")

D_MODEL = 1024
D_INNER = 2048
NH = 32
HD = 64
CONV_DIM = 2304
D_IN_PROJ = 4384
WPAD = 4384
DFF = 2816
L = 2048
NB = 2
NT = 16
NK = 8
NC18 = 18
EPS = 1e-5

MSK_RND = 0x200000
MSK_AND = 0xFFC00000 - (1 << 32)

ALL_PHASES = ("p0", "p1", "p2", "p3", "p3b", "p4a", "p4b")


class P:

    def __init__(self, nc, sems, mksem):
        self.nc = nc
        self.sems = dict(sems)
        self.mksem = mksem
        self.q = {e: [] for e in ENGS}
        self.cnt = {k: 0 for k in ENGS}
        self.wm = {e: {} for e in ENGS}

    def stream(self, name):
        key = "D:" + name
        if key not in self.sems:
            self.sems[key] = self.mksem(key)
            self.cnt[key] = 0
        return key

    def _wait(self, eng, key, v):
        if v <= 0:
            return
        if key == eng:
            pass
        if v > self.wm[eng].get(key, 0):
            assert v <= self.cnt[key], f"wait-ahead {eng}<-{key}: {v}>{self.cnt[key]}"
            self.wm[eng][key] = v
            s = self.sems[key]
            self.q[eng].append(lambda e, s=s, v=v: e.wait_ge(s, v))

    def op(self, eng, fn, deps=(), inc=1, produce=True):
        for d in deps:
            if not d:
                continue
            self._wait(eng, d[0], d[1])
        if produce:
            self.cnt[eng] += inc
            s = self.sems[eng]
            self.q[eng].append(
                lambda e, fn=fn, s=s, i=inc: fn(e).then_inc(s, i))
            return (eng, self.cnt[eng])
        self.q[eng].append(lambda e, fn=fn: fn(e))
        return (eng, self.cnt[eng])

    def dma(self, eng, dst, src, deps=(), stream=None):
        assert stream is not None
        key = self.stream(stream)
        for d in deps:
            if not d:
                continue
            self._wait(eng, d[0], d[1])
        self.cnt[key] += 16
        s = self.sems[key]
        self.q[eng].append(
            lambda e, s=s, dst=dst, src=src: e.dma_start(
                out=dst, in_=src).then_inc(s, 16))
        return (key, self.cnt[key])

    def dmat(self, dst, src, deps=(), stream=None):
        assert stream is not None
        key = self.stream(stream)
        for d in deps:
            if not d:
                continue
            self._wait("sync", d[0], d[1])
        self.cnt[key] += 16
        s = self.sems[key]
        self.q["sync"].append(
            lambda e, s=s, dst=dst, src=src: e.dma_start_transpose(
                dst, src).then_inc(s, 16))
        return (key, self.cnt[key])

    def nop(self, eng):
        self.op(eng, lambda e: e.engine_nop(), (), produce=False)

    def barrier(self):
        toks = [(k, self.cnt[k]) for k in self.cnt]
        for e in ENGS:
            for k, v in toks:
                if k == e:
                    continue
                self._wait(e, k, v)

    def finalize(self, block):
        @block.sync
        def _(e):
            for f in self.q["sync"]:
                f(e)

        @block.gpsimd
        def _(e):
            for f in self.q["gpsimd"]:
                f(e)

        @block.vector
        def _(e):
            for f in self.q["vector"]:
                f(e)

        @block.scalar
        def _(e):
            for f in self.q["scalar"]:
                f(e)

        @block.tensor
        def _(e):
            for f in self.q["tensor"]:
                f(e)


def cdiv(a, b):
    return (a + b - 1) // b


def build(nc, dbg=(), phases=ALL_PHASES):
    import contextlib
    dbg = set(dbg)
    phases = set(phases)

    def din(name, shape):
        return nc.dram_tensor(name, list(shape), F32, kind="ExternalInput").ap()

    x_in = din("x", (NB, L, D_MODEL))
    w_norm = din("norm_w", (D_MODEL,))
    w_inp = din("in_proj_w", (D_IN_PROJ, D_MODEL))
    w_conv = din("conv_w", (CONV_DIM, 1, 4))
    b_conv = din("conv_b", (CONV_DIM,))
    dt_bias = din("dt_bias", (NH,))
    a_log = din("A_log", (NH,))
    d_in = din("D", (NH,))
    w_gnorm = din("gnorm_w", (D_INNER,))
    w_out = din("out_proj_w", (D_MODEL, D_INNER))
    w_gate = din("ffn_w_gate", (DFF, D_MODEL))
    w_up = din("ffn_w_up", (DFF, D_MODEL))
    w_down = din("ffn_w_down", (D_MODEL, DFF))

    def scr_dram(name, shape, dtype, dbgkey=None, shared=False):
        kind = "ExternalOutput" if (dbgkey and dbgkey in dbg) else "Internal"
        kw = {}
        return nc.dram_tensor(name, list(shape), dtype, kind=kind, **kw).ap()

    winp_bf = scr_dram("winp_bf", (D_IN_PROJ, D_MODEL), BF16, "winp")
    wout_bf = scr_dram("wout_bf", (D_MODEL, D_INNER), BF16, "wout")
    qg_bf = scr_dram("qg_bf", (DFF, D_MODEL), BF16, "qg", shared=True)
    qu_bf = scr_dram("qu_bf", (DFF, D_MODEL), BF16, "qu", shared=True)
    qd_bf = scr_dram("qd_bf", (D_MODEL, DFF), BF16, "qd", shared=True)
    xn_sp = scr_dram("xn_sp", (NB * L, D_MODEL), F32, "xn")
    xdbg = scr_dram("xdbg", (NB * L, D_MODEL), F32, "p1dbg") if "p1dbg" in dbg else None
    sdbg = scr_dram("sdbg", (NB * NT, 128, 8), F32, "p1dbg") if "p1dbg" in dbg else None
    u_sp = scr_dram("u_sp", (NB * L, D_MODEL), BF16, "u")
    xq_sp = scr_dram("xq_sp", (NB * L, D_MODEL), BF16, "xq")
    xs_sp = scr_dram("xs_sp", (NB, D_INNER, L), BF16, "xs")
    sz_sp = scr_dram("sz_sp", (NB * L, D_INNER), BF16, "sz")
    dt_dbg = scr_dram("dt_dbg", (NB, 128, NT * NH), F32, "dt") if "dt" in dbg else None
    yb_sp = scr_dram("yb_sp", (NB * L, D_INNER), BF16, "yb")
    s_sp = scr_dram("s_sp", (NB * L, D_MODEL), F32, "s")
    hq_sp = scr_dram("hq_sp", (NB * L, DFF), BF16, "hq")
    g_dbg = scr_dram("g_dbg", (NB * L, D_MODEL), F32, "g") if "g" in dbg else None
    out = nc.dram_tensor("out", [NB, L, D_MODEL], F32, kind="ExternalOutput").ap()

    sb = nc.alloc_sbuf_tensor
    ident = sb("ident", [128, 128], BF16)
    u1f = sb("u1f", [128, 128], F32)
    u1b = sb("u1b", [128, 128], BF16)
    ones_b = sb("ones_b", [128, 128], BF16)
    ones_f = sb("ones_f", [128, 128], F32)
    dtb_rep = sb("dtb_rep", [128, NH], F32)
    a_rep = sb("a_rep", [128, NH], F32)
    d_rep = sb("d_rep", [128, NH], BF16)
    convw = sb("convw", [128, NC18 * 4], F32)
    convb = sb("convb", [128, NC18], F32)
    bt_t = [sb(f"bt{b}", [128, L], BF16) for b in range(NB)]
    ct_t = [sb(f"ct{b}", [128, L], BF16) for b in range(NB)]
    dt_sp = [sb(f"dtsp{b}", [128, NT * NH], F32) for b in range(NB)]
    st = {n: sb(f"st_{n}", [128, 1], F32)
          for n in ("xsum", "m", "mm", "va", "ssq", "rstd", "ssq2", "rstd2",
                    "ssq3", "l3", "rstd3")}
    junkq = sb("junkq", [128, D_MODEL], BF16)
    dbg8 = sb("dbg8", [128, 8], F32)
    p1xa = sb("p1xa", [128, D_MODEL], F32)
    p1xb = sb("p1xb", [128, D_MODEL], F32)
    p1xnt = sb("p1xnt", [128, D_MODEL], F32)
    p1ut = sb("p1ut", [128, D_MODEL], BF16)

    psA = nc.alloc_psum_tensor("psA", [128, 2048], F32)
    psB = nc.alloc_psum_tensor("psB", [128, 2048], F32)
    psBb = psB.ap().bitcast(BF16)

    with contextlib.ExitStack() as stk, nc.Block() as block:
        sems = {e: stk.enter_context(nc.semaphore(f"sem_{e}")) for e in ENGS}
        p = P(nc, sems, lambda nm: stk.enter_context(nc.semaphore(nm)))
        T = {}

        def alloc_group(stk2, specs):
            return {name: stk2.enter_context(
                nc.sbuf_tensor(name, list(shape), dtype))
                for name, shape, dtype in specs}

        def quant(sx, dst, W, scr, deps, chain="g"):
            ck = "qscr_last_" + chain
            deps = tuple(deps) + (T.get(ck),)
            nb16 = W // 16
            red, red6, rcl = scr["red"], scr["red6"], scr["rcl"]
            ax, v, t1i, i1 = scr["ax"], scr["v"], scr["t1i"], scr["i1"]
            ap_, qa, sgn = scr["ap"], scr["qa"], scr["sgn"]
            rows = sx.shape[0]
            x3 = sx.rearrange("p (a b) -> p a b", b=16)
            p.nop("vector")
            t_red = p.op("vector", lambda e: e.reduce_max(
                red, x3, AX.X, apply_absolute_value=True), deps)
            p.nop("vector")
            t_red6 = p.op("vector", lambda e: e.tensor_scalar(
                red6, red, 6e-12, 1.0 / 6.0, ALU.max, ALU.mult), (t_red,))
            p.nop("vector")
            t_rcl = p.op("vector", lambda e: e.reciprocal(rcl, red6), (t_red6,))
            t_ax = p.op("scalar", lambda e: e.activation(ax, sx, AF.Abs), deps)
            t_sgn = p.op("scalar", lambda e: e.activation(sgn, sx, AF.Sign), deps)
            rb = rcl.unsqueeze(2).broadcast_to((rows, nb16, 16))
            a3 = ax.rearrange("p (a b) -> p a b", b=16)
            v3 = v.rearrange("p (a b) -> p a b", b=16)
            t_v = p.op("gpsimd", lambda e: e.tensor_tensor(v3, a3, rb, ALU.mult),
                       (t_ax, t_rcl))
            t_t1 = p.op("vector", lambda e: e.tensor_scalar(
                t1i, v, 2.0, None, ALU.mult), (t_v,))
            t_i1a = p.op("vector", lambda e: e.tensor_scalar(
                i1, v.bitcast(I32), MSK_RND, None, ALU.add), (t_v,))
            t_ap = p.op("vector", lambda e: e.tensor_scalar(
                ap_, t1i, 4.0, 0.5, ALU.min, ALU.mult), (t_t1,))
            t_i1b = p.op("vector", lambda e: e.tensor_scalar(
                i1, i1, MSK_AND, None, ALU.bitwise_and), (t_i1a,))
            p.nop("vector")
            t_bm = p.op("vector", lambda e: e.scalar_tensor_tensor(
                v, i1.bitcast(F32), 3.0, i1.bitcast(F32),
                ALU.is_ge, ALU.mult), (t_i1b,))
            t_qa = p.op("vector", lambda e: e.tensor_tensor(
                qa, ap_, v, ALU.max), (t_ap, t_bm))
            p.nop("vector")
            t_qs = p.op("vector", lambda e: e.tensor_tensor(
                qa, qa, sgn, ALU.mult), (t_qa, t_sgn))
            r6b = red6.unsqueeze(2).broadcast_to((rows, nb16, 16))
            q3 = qa.rearrange("p (a b) -> p a b", b=16)
            d3 = dst.rearrange("p (a b) -> p a b", b=16)
            t_qb = p.op("gpsimd", lambda e: e.tensor_tensor(d3, q3, r6b, ALU.mult),
                        (t_qs, t_red6))
            T[ck] = t_qb
            return t_qb

        c_eps = nc.alloc_sbuf_tensor("c_eps", [128, 1], F32)
        c_one = nc.alloc_sbuf_tensor("c_one", [128, 1], F32)
        p.op("gpsimd", lambda e: e.memset(c_eps[:, :], EPS))
        p.op("gpsimd", lambda e: e.memset(c_one[:, :], 1.0))
        nc.const_aps.register(F32, EPS, c_eps.ap())
        nc.const_aps.register(F32, 1.0, c_one.ap())
        t_id0 = p.op("gpsimd", lambda e: e.memset(ident[:, :], 0.0))
        t_id = p.op("gpsimd", lambda e: e.affine_select(
            out=ident[:, :], in_=ident[:, :], compare_op=ALU.not_equal, fill=1.0,
            base=0, pattern=[[-1, 128]], channel_multiplier=1), (t_id0,))
        t_u10 = p.op("gpsimd", lambda e: e.memset(u1f[:, :], 1.0))
        t_u1 = p.op("gpsimd", lambda e: e.affine_select(
            out=u1f[:, :], in_=u1f[:, :], compare_op=ALU.is_ge, fill=0.0,
            base=0, pattern=[[1, 128]], channel_multiplier=-1), (t_u10,))
        t_u1b = p.op("vector", lambda e: e.tensor_copy(u1b[:, :], u1f[:, :]),
                     (t_u1,))
        t_onesb = p.op("gpsimd", lambda e: e.memset(ones_b[:, :], 1.0))
        t_onesf = p.op("gpsimd", lambda e: e.memset(ones_f[:, :], 1.0))

        t_convw = None
        for c in range(NC18):
            t_convw = p.dma(
                "sync", convw.ap().rearrange("p (t k) -> p t k", k=4)[:, c],
                w_conv.rearrange("(t p) o k -> t p (o k)", p=128)[c],
                stream="cwb")
        t_convb = None
        for c in range(NC18):
            t_convb = p.dma("sync", convb[:, c:c + 1],
                            b_conv.rearrange("(t p o) -> t p o", p=128, o=1)[c],
                            stream="cwb")
        t_convw = t_convb

        import contextlib as _cl0
        with _cl0.ExitStack() as ph0:
            qstk = ph0.enter_context(_cl0.ExitStack())
            g0_ = alloc_group(qstk, [
                ("norm_rep", (128, D_MODEL), F32),
                ("gnorm_rep", (128, D_INNER), F32)])
            norm_rep = g0_["norm_rep"]; gnorm_rep = g0_["gnorm_rep"]

            def alloc_qscr(stk_, tag, W):
                g = alloc_group(stk_, [
                    (tag + "red", (128, W // 16), F32),
                    (tag + "red6", (128, W // 16), F32),
                    (tag + "ax", (128, W), F32), (tag + "v", (128, W), F32),
                    (tag + "i1", (128, W), I32),
                    (tag + "ap", (128, W), BF16),
                    (tag + "sgn", (128, W), BF16)])

                def sl(ncols):
                    return dict(
                        red=g[tag + "red"][:, :ncols // 16],
                        red6=g[tag + "red6"][:, :ncols // 16],
                        rcl=g[tag + "red"][:, :ncols // 16],
                        ax=g[tag + "ax"][:, :ncols], v=g[tag + "v"][:, :ncols],
                        t1i=g[tag + "ax"][:, :ncols].bitcast(I32),
                        i1=g[tag + "i1"][:, :ncols],
                        ap=g[tag + "ap"][:, :ncols],
                        qa=g[tag + "i1"][:, :ncols // 2].bitcast(BF16),
                        sgn=g[tag + "sgn"][:, :ncols])
                return sl

            gq_ = alloc_group(qstk, [
                ("qw_w0", (128, 1024), F32), ("qw_b0", (128, 1024), BF16),
                ("qw_w1", (128, 1024), F32), ("qw_b1", (128, 1024), BF16)])
            gqsl = alloc_qscr(qstk, "qwA", 1024)
            qlast = {"w": [None, None], "b": [None, None], "i": 0}
            qunits = []

            def _mk_qunit(srcap, dstap, ncols, extra=None):
                def emit():
                    pi = qlast["i"] % 2
                    qlast["i"] += 1
                    wt = gq_[f"qw_w{pi}"]
                    bt = gq_[f"qw_b{pi}"]
                    tl = p.dma("sync", wt[:, :ncols], srcap,
                               (qlast["w"][pi], extra), stream=f"wt{pi}")
                    tq = quant(wt[:, :ncols], bt[:, :ncols],
                               ncols, gqsl(ncols), (tl, qlast["b"][pi]),
                               chain="w")
                    qlast["w"][pi] = tq
                    qlast["b"][pi] = p.dma("gpsimd", dstap, bt[:, :ncols],
                                           (tq,), stream=f"btile{pi}")
                return emit

            def emit_qunit():
                if qunits:
                    qunits.pop(0)()

            T["psB"] = None

            rbstk = _cl0.ExitStack()
            grb_ = alloc_group(rbstk, [
                ("rowbuf", (1, D_INNER), F32),
                ("dtmp", (128, NH), F32)])
            rowbuf = grb_["rowbuf"]; dtmp = grb_["dtmp"]

            def replicate_row(dst_sb, src_row_ap, width):
                tok_l = p.dma("sync", rowbuf[:, :width], src_row_ap,
                              (T.get("rowbuf_free"),), stream="row")
                mts = []
                for j0 in range(0, width, 512):
                    w = min(512, width - j0)
                    mts.append(p.op("tensor", lambda e, j0=j0, w=w: e.matmul(
                        psB[:, j0:j0 + w], ones_f[0:1, :], rowbuf[:, j0:j0 + w],
                        start=True, stop=True), (tok_l, t_onesf, T["psB"])))
                tok_e = p.op("vector", lambda e: e.tensor_copy(
                    dst_sb[:, :width], psB[:, :width]), tuple(mts))
                T["psB"] = tok_e
                T["rowbuf_free"] = mts[-1]
                return tok_e

            t_nr = replicate_row(norm_rep, w_norm.rearrange("(a b) -> a b", a=1),
                                 D_MODEL)
            t_gr = replicate_row(gnorm_rep,
                                 w_gnorm.rearrange("(a b) -> a b", a=1), D_INNER)
            t_dtb = replicate_row(dtb_rep, dt_bias.rearrange("(a b) -> a b", a=1),
                                  NH)
            t_dr = replicate_row(dtmp, d_in.rearrange("(a b) -> a b", a=1), NH)
            t_drb = p.op("vector", lambda e: e.tensor_copy(
                d_rep[:, :], dtmp[:, :]), (t_dr,))
            t_ar = replicate_row(a_rep, a_log.rearrange("(a b) -> a b", a=1), NH)
            t_ar2 = p.op("scalar", lambda e: e.activation(
                a_rep[:, :], a_rep[:, :], AF.Exp), (t_ar,))
            t_ar3 = p.op("vector", lambda e: e.tensor_scalar(
                a_rep[:, :], a_rep[:, :], -1.0, None, ALU.mult), (t_ar2,))
            T["a_rep"] = t_ar3
            T["dtb_rep"] = t_dtb
            T["d_rep"] = t_drb
            T["rb_done"] = (t_ar3, t_drb)
            rbstk.close()

            if "p0" in phases:
                import contextlib as _cl
                with _cl.ExitStack() as ph:
                    g_ = alloc_group(ph, [
                        ("p0w", (128, DFF), F32), ("p0b", (128, DFF), BF16),
                        ("p0red", (128, DFF // 16), F32),
                        ("p0red6", (128, DFF // 16), F32),
                        ("p0rcl", (128, DFF // 16), F32),
                        ("p0ax", (128, DFF), F32), ("p0v", (128, DFF), F32),
                        ("p0t1i", (128, DFF), I32), ("p0i1", (128, DFF), I32),
                        ("p0ap", (128, DFF), BF16), ("p0msk", (128, DFF), BF16),
                        ("p0qa", (128, DFF), BF16), ("p0sgn", (128, DFF), BF16)])
                    wtiles = [g_["p0w"], g_["p0ax"]]
                    btiles = [g_["p0b"], g_["p0ap"]]
                    last = {"w": list(T["rb_done"]), "b": [None, None], "i": 0}

                    def prep_fold(src, dst, nrows, ncols, rep, rep_tok):
                        for i in range(cdiv(nrows, 128)):
                            pi = last["i"] % 2
                            last["i"] += 1
                            wtile = wtiles[pi]
                            btile = btiles[pi]
                            r0 = i * 128
                            rr = min(128, nrows - r0)
                            tl = p.dma("sync", wtile[0:rr, :ncols],
                                       src[r0:r0 + rr, :], (last["w"][pi],),
                                       stream=f"pf{pi}")
                            tf = p.op("vector", lambda e, rr=rr, ncols=ncols,
                                      wtile=wtile, btile=btile:
                                      e.tensor_tensor(
                                          btile[0:rr, :ncols],
                                          wtile[0:rr, :ncols],
                                          rep[0:rr, :ncols], ALU.mult),
                                      (tl, rep_tok, last["b"][pi]))
                            last["w"][pi] = tf
                            last["b"][pi] = p.dma("gpsimd", dst[r0:r0 + rr, :],
                                                  btile[0:rr, :ncols], (tf,),
                                                  stream=f"pb{pi}")

                    prep_fold(w_inp, winp_bf, D_IN_PROJ, D_MODEL, norm_rep, t_nr)
                    T["winp_done"] = (last["b"][0], last["b"][1])
                    prep_fold(w_out, wout_bf, D_MODEL, D_INNER, gnorm_rep, t_gr)

                    for i in range(22):
                        qunits.append(_mk_qunit(
                            w_gate[i * 128:(i + 1) * 128, :],
                            qg_bf[i * 128:(i + 1) * 128, :], D_MODEL))
                    for i in range(22):
                        qunits.append(_mk_qunit(
                            w_up[i * 128:(i + 1) * 128, :],
                            qu_bf[i * 128:(i + 1) * 128, :], D_MODEL))
                    for i in range(8):
                        for c0, cw in ((0, 1024), (1024, 1024), (2048, 768)):
                            qunits.append(_mk_qunit(
                                w_down[i * 128:(i + 1) * 128, c0:c0 + cw],
                                qd_bf[i * 128:(i + 1) * 128, c0:c0 + cw], cw))
                    T["p0done"] = p.op(
                        "gpsimd", lambda e: e.memset(dbg8[:, 0:1], 0.0),
                        (last["b"][0], last["b"][1]))

            if "p2" in phases:
                import contextlib as _cl
                with _cl.ExitStack() as ph:
                    g_ = alloc_group(ph, [
                        ("uT", (128, NK * L), BF16),
                        ("winpT", (128, NK * WPAD), BF16),
                        ("preact", (128, L + 4), BF16),
                        ("accA", (128, L), BF16), ("accB", (128, L), BF16),
                        ("zst", (128, 512), BF16)])
                    uT = g_["uT"]; wT = g_["winpT"]; pre = g_["preact"]
                    accA = g_["accA"]; accB = g_["accB"]
                    zst = g_["zst"]
                    xbufs = [p1xa, p1xb]
                    xnt = p1xnt; ut = p1ut
                    p1s = {"cons": [None, None], "xn_cons": (None, None)}
                    T.setdefault("pre_free", T.get("p0done"))

                    def emit_p1_tile(i):
                        b, t = divmod(i, NT)
                        xb = xbufs[i % 2]
                        r0 = b * L + t * 128
                        tl = p.dma("sync", xb[:, :],
                                   x_in[b, t * 128:(t + 1) * 128, :],
                                   (p1s["cons"][i % 2],), stream=f"x{i % 2}")
                        t_sum = p.op("vector", lambda e, xb=xb: e.reduce_sum(
                            st["xsum"][:, :], xb[:, :], AX.X), (tl,))
                        t_sq = p.op("scalar", lambda e, xb=xb: e.activation(
                            junkq[:, :D_MODEL], xb[:, :], AF.Square,
                            accum_out=st["ssq"][:, :]), (tl,))
                        p.nop("vector")
                        t_m = p.op("vector", lambda e: e.tensor_scalar(
                            st["m"][:, :], st["xsum"][:, :], 1.0 / D_MODEL,
                            None, ALU.mult), (t_sum,))
                        p.nop("vector")
                        t_mm = p.op("vector", lambda e: e.tensor_tensor(
                            st["mm"][:, :], st["m"][:, :], st["m"][:, :],
                            ALU.mult), (t_m,))
                        p.nop("vector")
                        t_va = p.op("vector", lambda e: e.scalar_tensor_tensor(
                            st["va"][:, :], st["ssq"][:, :], 1.0 / D_MODEL,
                            st["mm"][:, :], ALU.mult, ALU.subtract),
                            (t_sq, t_mm))
                        t_l = p.op("scalar", lambda e: e.activation(
                            st["va"][:, :], st["va"][:, :], AF.Ln, bias=EPS),
                            (t_va,))
                        t_rstd = p.op("scalar", lambda e: e.activation(
                            st["rstd"][:, :], st["va"][:, :], AF.Exp,
                            scale=-0.5), (t_l,))
                        t_xn = p.op("vector", lambda e, xb=xb: e.tensor_scalar(
                            xnt[:, :], xb[:, :], st["m"][:, :],
                            st["rstd"][:, :], ALU.subtract, ALU.mult),
                            (t_rstd, t_m) + p1s["xn_cons"])
                        p1s["cons"][i % 2] = t_xn
                        t_spxn = p.dma("gpsimd", xn_sp[r0:r0 + 128, :],
                                       xnt[:, :], (t_xn,), stream="xnt")
                        t_sq2 = p.op("scalar", lambda e: e.activation(
                            junkq[:, :D_MODEL], xnt[:, :], AF.Square,
                            accum_out=st["ssq2"][:, :]), (t_xn,))
                        t_l2 = p.op("scalar", lambda e: e.activation(
                            st["ssq2"][:, :], st["ssq2"][:, :], AF.Ln,
                            scale=1.0 / D_MODEL, bias=EPS), (t_sq2,))
                        t_rstd2 = p.op("scalar", lambda e: e.activation(
                            st["rstd2"][:, :], st["ssq2"][:, :], AF.Exp,
                            scale=-0.5), (t_l2,))
                        t_u = p.op("vector", lambda e: e.tensor_scalar(
                            ut[:, :], xnt[:, :], st["rstd2"][:, :], None,
                            ALU.mult), (t_rstd2, t_xn, T.get("ut_free")))
                        t_spu = p.dma("gpsimd", u_sp[r0:r0 + 128, :], ut[:, :],
                                      (t_u,), stream="ut")
                        T["ut_free"] = t_spu
                        T[f"u_done{b}"] = t_spu
                        p1s["xn_cons"] = (t_spxn, t_u)
                        qunits.append(_mk_qunit(
                            xn_sp[r0:r0 + 128, :], xq_sp[r0:r0 + 128, :],
                            D_MODEL, extra=t_spxn))

                    t_wT = []
                    wT_deps = (T.get("p0done"),)
                    for k in range(NK):
                        p.dmat(wT[:, k * WPAD:k * WPAD + 4352],
                               winp_bf[0:4352, k * 128:(k + 1) * 128],
                               deps=wT_deps if k == 0 else (),
                               stream="wT")
                        t_wT.append(p.dmat(
                            wT[:, k * WPAD + 4352:(k + 1) * WPAD],
                            winp_bf[4352:4384, k * 128:(k + 1) * 128],
                            stream="wT"))
                    t_wT = [t_wT[-1]] * NK

                    p1_pending = list(range(NB * NT))
                    for _ in range(NT):
                        emit_p1_tile(p1_pending.pop(0))

                    def feed():
                        if p1_pending:
                            emit_p1_tile(p1_pending.pop(0))
                        else:
                            emit_qunit()
                    for b in range(NB):
                        if b == 1:
                            while p1_pending:
                                emit_p1_tile(p1_pending.pop(0))
                        t_uT = []
                        for k in range(NK):
                            t_uT.append(p.dmat(
                                uT[:, k * L:(k + 1) * L],
                                u_sp[b * L:(b + 1) * L, k * 128:(k + 1) * 128],
                                (T.get("uT_free"), T.get(f"u_done{b}")),
                                stream="uT"))
                        t_uT = [t_uT[-1]] * NK
                        bankB = [T.get("psB")] * 4
                        for t in range(NT):
                            zevs = []
                            for j in range(4):
                                sl = psB[:, j * 512:(j + 1) * 512]
                                mts = []
                                for k in range(NK):
                                    mts.append(p.op("tensor",
                                        lambda e, sl=sl, k=k, t=t, j=j: e.matmul(
                                            sl,
                                            uT[:, k * L + t * 128:
                                               k * L + (t + 1) * 128],
                                            wT[:, k * WPAD + j * 512:
                                               k * WPAD + (j + 1) * 512],
                                            start=(k == 0), stop=(k == NK - 1)),
                                        (t_wT[k], t_uT[k], bankB[j])))
                                ev = p.op("scalar", lambda e, sl=sl, j=j:
                                          e.activation(
                                              zst[:, :], sl, AF.Silu),
                                          (mts[-1], T.get("zst_free")))
                                bankB[j] = ev
                                r0 = b * L + t * 128
                                t_zs = p.dma(
                                    "gpsimd",
                                    sz_sp[r0:r0 + 128, j * 512:(j + 1) * 512],
                                    zst[:, :], (ev,), stream="zst")
                                T["zst_free"] = t_zs
                                zevs.append(ev)
                            if b == 0:
                                feed()
                            else:
                                emit_qunit()
                        bankA = [T.get("psA")] * 4
                        for c in range(NC18):
                            evs = []
                            t_pad = p.op("gpsimd", lambda e: e.memset(
                                pre[:, 0:3], 0.0), (T.get("pre_free"),))
                            for j in range(4):
                                sl = psA[:, j * 512:(j + 1) * 512]
                                mts = []
                                for k in range(NK):
                                    fcol = D_INNER + c * 128
                                    mts.append(p.op("tensor",
                                        lambda e, sl=sl, k=k, c=c, j=j,
                                        fcol=fcol: e.matmul(
                                            sl,
                                            wT[:, k * WPAD + fcol:
                                               k * WPAD + fcol + 128],
                                            uT[:, k * L + j * 512:
                                               k * L + (j + 1) * 512],
                                            start=(k == 0), stop=(k == NK - 1)),
                                        (t_wT[k], t_uT[k], bankA[j])))
                                ev = p.op("vector", lambda e, sl=sl, j=j:
                                          e.tensor_copy(
                                              pre[:, 3 + j * 512:
                                                  3 + (j + 1) * 512], sl),
                                          (mts[-1],))
                                bankA[j] = ev
                                evs.append(ev)
                            p.nop("vector")
                            t_t0 = p.op("vector", lambda e, c=c: e.tensor_scalar(
                                accA[:, :], pre[:, 0:L],
                                convw[:, 4 * c:4 * c + 1], None, ALU.mult),
                                tuple(evs) + (t_pad, t_convw,
                                              T.get("accA_free")))
                            p.nop("vector")
                            t_t1 = p.op("vector", lambda e, c=c:
                                        e.scalar_tensor_tensor(
                                            accB[:, :], pre[:, 1:L + 1],
                                            convw[:, 4 * c + 1:4 * c + 2],
                                            accA[:, :], ALU.mult, ALU.add),
                                        (t_t0, T.get("accB_free")))
                            p.nop("vector")
                            t_t2 = p.op("vector", lambda e, c=c:
                                        e.scalar_tensor_tensor(
                                            accA[:, :], pre[:, 2:L + 2],
                                            convw[:, 4 * c + 2:4 * c + 3],
                                            accB[:, :], ALU.mult, ALU.add),
                                        (t_t1,))
                            p.nop("vector")
                            t_t3 = p.op("vector", lambda e, c=c:
                                        e.scalar_tensor_tensor(
                                            accB[:, :], pre[:, 3:L + 3],
                                            convw[:, 4 * c + 3:4 * c + 4],
                                            accA[:, :], ALU.mult, ALU.add),
                                        (t_t2,))
                            T["pre_free"] = t_t3
                            T["accA_free"] = t_t3
                            dst = (accA[:, :] if c < 16 else
                                   (bt_t[b][:, :] if c == 16 else ct_t[b][:, :]))
                            t_act = p.op("scalar", lambda e, c=c, dst=dst:
                                         e.activation(dst, accB[:, :], AF.Silu,
                                                      bias=convb[:, c:c + 1]),
                                         (t_t3, t_convb,
                                          T.get("xbcst_free") if c < 16
                                          else None))
                            T["accB_free"] = t_act
                            if c < 16:
                                T["xbcst_free"] = p.dma(
                                    "gpsimd", xs_sp[b, c * 128:(c + 1) * 128, :],
                                    accA[:, :], (t_act,), stream="xbcst")
                                T["accA_free"] = T["xbcst_free"]
                            if b == 0:
                                feed()
                            else:
                                emit_qunit()
                        T["psA"] = bankA[3]
                        for t in range(NT):
                            sl = psB[:, (t % 4) * 512:(t % 4) * 512 + NH]
                            mts = []
                            for k in range(NK):
                                mts.append(p.op("tensor",
                                    lambda e, sl=sl, k=k, t=t: e.matmul(
                                        sl,
                                        uT[:, k * L + t * 128:
                                           k * L + (t + 1) * 128],
                                        wT[:, k * WPAD + 4352:
                                           k * WPAD + 4384],
                                        start=(k == 0), stop=(k == NK - 1)),
                                    (t_wT[k], t_uT[k], bankB[t % 4])))
                            ev = p.op("vector", lambda e, sl=sl, b=b, t=t:
                                      e.tensor_copy(
                                          dt_sp[b][:, t * NH:(t + 1) * NH], sl),
                                      (mts[-1],))
                            bankB[t % 4] = ev
                            emit_qunit()
                        T["psB"] = bankB[3]
                        T["uT_free"] = bankB[3]
                p.barrier()

            if "p3" in phases:
                import contextlib as _cl
                with _cl.ExitStack() as ph:
                    g_ = alloc_group(ph, [
                        ("xs0", (128, D_INNER), BF16),
                        ("xs1", (128, D_INNER), BF16),
                        ("szb0", (128, D_INNER), BF16),
                        ("szb1", (128, D_INNER), BF16),
                        ("sp0", (128, D_INNER), F32),
                        ("sp1", (128, D_INNER), F32),
                        ("spb", (128, D_INNER), BF16),
                        ("rhs1", (128, 4096), BF16),
                        ("mp", (128, 4096), BF16),
                        ("e1t", (128, D_INNER), BF16),
                        ("rlt", (128, D_INNER), F32),
                        ("xdt", (128, D_INNER), BF16),
                        ("xdd", (128, D_INNER), BF16),
                        ("t2b", (128, D_INNER), BF16),
                        ("t1f", (128, D_INNER), F32),
                        ("y1f", (128, D_INNER), F32),
                        ("y2f", (128, D_INNER), F32),
                        ("ygf", (128, D_INNER), F32),
                        ("ybt", (128, D_INNER), BF16),
                        ("acs", (32, 128), F32), ("acsT", (128, NH), F32),
                        ("dsb", (32, 128), BF16), ("dsT", (128, NH), BF16),
                        ("eat", (128, NH), F32), ("erow", (32, 1), F32),
                        ("ermat", (32, 32), F32), ("edr", (128, NH), F32),
                        ("gmt", (128, 128), BF16), ("bct", (128, 128), BF16),
                        ("aft", (128, NH), F32), ("abt", (128, NH), BF16),
                        ("nab", (128, NH), BF16),
                        ("spg1", (128, NT * NH), F32),
                        ("spg2", (128, NT * NH), F32)])
                    xs0 = g_["xs0"]; xs1 = g_["xs1"]
                    szb0 = g_["szb0"]; szb1 = g_["szb1"]
                    sp0 = g_["sp0"]; sp1 = g_["sp1"]; spb = g_["spb"]
                    rhs1 = g_["rhs1"]; mp = g_["mp"]; e1t = g_["e1t"]
                    rlt = g_["rlt"]; xdt = g_["xdt"]; xdd = g_["xdd"]
                    t2b = g_["t2b"]; t1f = g_["t1f"]; y1f = g_["y1f"]
                    y2f = g_["y2f"]; ygf = g_["ygf"]; ybt = g_["ybt"]
                    acs = g_["acs"]; acsT = g_["acsT"]; dsb = g_["dsb"]
                    dsT = g_["dsT"]; eat = g_["eat"]; erow = g_["erow"]
                    ermat = g_["ermat"]; edr = g_["edr"]; gmt = g_["gmt"]
                    bct = g_["bct"]; aft = g_["aft"]; abt = g_["abt"]
                    nab = g_["nab"]; spg1 = g_["spg1"]; spg2 = g_["spg2"]
                    for b in range(NB):
                        dtb_b = dtb_rep.ap().unsqueeze(1).broadcast_to((128, NT, NH))
                        dtr3 = dt_sp[b].ap().rearrange("p (t h) -> p t h", h=NH)
                        t_d0 = p.op("vector", lambda e, dtr3=dtr3, dtb_b=dtb_b:
                                    e.tensor_tensor(dtr3, dtr3, dtb_b, ALU.add),
                                    (T["dtb_rep"],))
                        t_g1 = p.op("scalar", lambda e, b=b: e.activation(
                            spg1[:, :], dt_sp[b][:, :], AF.Abs), (t_d0,))
                        t_g4 = p.op("scalar", lambda e, b=b: e.activation(
                            spg2[:, :], dt_sp[b][:, :], AF.Relu), (t_d0,))
                        t_g2 = p.op("scalar", lambda e: e.activation(
                            spg1[:, :], spg1[:, :], AF.Exp, scale=-1.0), (t_g1,))
                        t_g3 = p.op("scalar", lambda e: e.activation(
                            spg1[:, :], spg1[:, :], AF.Ln, bias=1.0), (t_g2,))
                        t_dt = p.op("vector", lambda e, b=b: e.tensor_tensor(
                            dt_sp[b][:, :], spg1[:, :], spg2[:, :], ALU.add),
                            (t_g3, t_g4))
                        if dt_dbg is not None:
                            p.dma("gpsimd", dt_dbg[b], dt_sp[b][:, :], (t_dt,),
                                  stream="dbg")
                        sps = [sp0, sp1]
                        t_spi = p.op("gpsimd", lambda e: e.memset(sp0[:, :], 0.0),
                                     (T.get("sp_free"),))
                        T["sp_prev_tok"] = t_spi
                        T["yoff_done"] = None
                        xsb = [xs0, xs1]
                        szb = [szb0, szb1]
                        t_xsl = [None, None]
                        t_szl = [None, None]
                        cons_xs = T.setdefault("cons_xs", [None, None])
                        cons_sz = T.setdefault("cons_sz", [None, None])
                        t_xsl[0] = p.dmat(xs0[:, :], xs_sp[b, :, 0:128],
                                          (cons_xs[0],), stream="xsl0")
                        t_szl[0] = p.dma("sync", szb0[:, :],
                                         sz_sp[b * L:b * L + 128, :],
                                         (cons_sz[0],), stream="szl0")
                        for c in range(NT):
                            i = c % 2
                            ni = (c + 1) % 2
                            if c + 1 < NT:
                                t_xsl[ni] = p.dmat(
                                    xsb[ni][:, :],
                                    xs_sp[b, :, (c + 1) * 128:(c + 2) * 128],
                                    (cons_xs[ni],), stream=f"xsl{ni}")
                                t_szl[ni] = p.dma(
                                    "sync", szb[ni][:, :],
                                    sz_sp[b * L + (c + 1) * 128:
                                          b * L + (c + 2) * 128, :],
                                    (cons_sz[ni],), stream=f"szl{ni}")
                            xs_c = xsb[i]
                            sz_c = szb[i]
                            spP = sps[c % 2]
                            spN = sps[(c + 1) % 2]
                            ccol = slice(c * 128, (c + 1) * 128)
                            dslice = dt_sp[b][:, c * NH:(c + 1) * NH]
                            t_af = p.op("vector", lambda e, dslice=dslice:
                                        e.tensor_tensor(aft[:, :], dslice,
                                                        a_rep[:, :], ALU.mult),
                                        (t_dt, T["a_rep"], T.get("af_free")))
                            p.nop("vector")
                            t_ab = p.op("vector", lambda e: e.tensor_copy(
                                abt[:, :], aft[:, :]), (t_af,))
                            t_nab = p.op("vector", lambda e: e.tensor_scalar(
                                nab[:, :], aft[:, :], -1.0, None, ALU.mult),
                                (t_af, T.get("nab_free")))
                            t_m1 = p.op("tensor", lambda e: e.matmul(
                                psB[0:32, 0:128], aft[:, :], u1f[:, :],
                                start=True, stop=True),
                                (t_af, t_u1, T.get("psB")))
                            t_m2 = p.op("tensor", lambda e: e.matmul(
                                psB[:, 128:128 + NH], u1f[:, :], aft[:, :],
                                start=True, stop=True), (t_af, t_u1))
                            t_acs = p.op("scalar", lambda e: e.activation(
                                acs[:, :], psB[0:32, 0:128], AF.Copy),
                                (t_m1, t_m2, T.get("acs_free")))
                            t_acsT = p.op("scalar", lambda e: e.activation(
                                acsT[:, :], psB[:, 128:128 + NH], AF.Copy),
                                (t_m2, T.get("acsT_free")))
                            T["af_free"] = t_m2
                            t_ds = p.op("scalar", lambda e: e.activation(
                                dsb[:, :], acs[:, :], AF.Exp, scale=-1.0,
                                bias=acs[:, 127:128]),
                                (t_acs, T.get("dsb_free")))
                            t_eat = p.op("scalar", lambda e: e.activation(
                                eat[:, :], acsT[:, :], AF.Exp),
                                (t_acsT, T.get("eat_free")))
                            t_er = p.op("scalar", lambda e: e.activation(
                                erow[:, :], acs[:, 127:128], AF.Exp),
                                (t_acs, T.get("erow_free")))
                            t_em0 = p.op("gpsimd", lambda e: e.memset(
                                ermat[:, :], 0.0), (T.get("ermat_free"),))
                            erb = erow.ap().broadcast_to((32, 32))
                            t_em = p.op("gpsimd", lambda e, erb=erb:
                                        e.affine_select(
                                            out=ermat[:, :], in_=erb,
                                            compare_op=ALU.is_equal, fill=0.0,
                                            base=0, pattern=[[-1, 32]],
                                            channel_multiplier=1),
                                        (t_er, t_em0))
                            T["erow_free"] = t_em
                            t_edm = p.op("tensor", lambda e: e.matmul(
                                psB[:, 160:160 + NH], ones_f[0:32, :],
                                ermat[:, :], start=True, stop=True),
                                (t_em, t_onesf, t_acsT))
                            T["ermat_free"] = t_edm
                            t_edr = p.op("vector", lambda e: e.tensor_copy(
                                edr[:, :], psB[:, 160:160 + NH]), (t_edm,))
                            t_gmm = p.op("tensor", lambda e, b=b, ccol=ccol:
                                         e.matmul(psB[:, 256:384],
                                                  bt_t[b][:, ccol],
                                                  ct_t[b][:, ccol],
                                                  start=True, stop=True),
                                         (t_edr,))
                            t_gev = p.op("vector", lambda e: e.tensor_copy(
                                gmt[:, :], psB[:, 256:384]),
                                (t_gmm, T.get("gmask_free")))
                            t_gmask = p.op("gpsimd", lambda e: e.affine_select(
                                out=gmt[:, :], in_=gmt[:, :],
                                compare_op=ALU.is_ge, fill=0.0, base=0,
                                pattern=[[1, 128]], channel_multiplier=-1),
                                (t_gev,))
                            T["gmask_free"] = t_gmask
                            t_dst_mm = p.op("tensor", lambda e: e.transpose(
                                psBb[:, 384:384 + NH], dsb[:, :],
                                ident[0:32, 0:32]), (t_ds, t_id, t_gev))
                            T["dsb_free"] = t_dst_mm
                            t_dsT = p.op("vector", lambda e: e.tensor_copy(
                                dsT[:, :], psBb[:, 384:384 + NH]), (t_dst_mm,))
                            ab3 = abt.ap().unsqueeze(2).broadcast_to(
                                (128, NH, 128))
                            u13 = u1b.ap().unsqueeze(1).broadcast_to(
                                (128, NH, 128))
                            r13 = rhs1.ap().rearrange("p (h l) -> p h l", l=128)
                            t_r1 = p.op("gpsimd", lambda e, ab3=ab3, u13=u13,
                                        r13=r13: e.tensor_tensor(
                                            r13, ab3, u13, ALU.mult),
                                        (t_ab, t_u1b, T.get("rhs1_free")))
                            t_mp = [None, None]
                            t_relu_prev = T.get("psA")
                            t_mm2_last = None
                            t_mp_prevgrp = T.get("mp_free")
                            t_e1_prev = T.get("e1_free")
                            for g in range(2):
                                mts = []
                                for j in range(4):
                                    mts.append(p.op("tensor",
                                        lambda e, g=g, j=j: e.matmul(
                                            psA[:, j * 512:(j + 1) * 512],
                                            ones_b[:, :],
                                            rhs1[:, g * 2048 + j * 512:
                                                 g * 2048 + (j + 1) * 512],
                                            start=True, stop=False),
                                        (t_r1, t_onesb, t_relu_prev)))
                                for j in range(4):
                                    h0 = g * 16 + j * 4
                                    nb3 = nab[:, h0:h0 + 4].unsqueeze(2).broadcast_to(
                                            (128, 4, 128))
                                    t_mm2_last = p.op("tensor",
                                        lambda e, j=j, nb3=nb3: e.matmul(
                                            psA[:, j * 512:(j + 1) * 512],
                                            u1b[:, :], nb3,
                                            start=False, stop=True),
                                        (t_nab, t_u1b))
                                    mts.append(t_mm2_last)
                                t_rl = p.op("scalar", lambda e: e.activation(
                                    rlt[:, :], psA[:, :], AF.Relu, scale=-1.0),
                                    tuple(mts))
                                t_e1 = p.op("scalar", lambda e: e.activation(
                                    e1t[:, :], rlt[:, :], AF.Exp, scale=-1.0),
                                    (t_rl, t_mp_prevgrp, t_e1_prev))
                                t_e1_prev = t_e1
                                gm3 = gmt.ap().unsqueeze(1).broadcast_to((128, 16, 128))
                                e13 = e1t.ap().rearrange(
                                    "p (h l) -> p h l", l=128)
                                mp3 = mp[:, g * 2048:(g + 1) * 2048].rearrange(
                                    "p (h l) -> p h l", l=128)
                                t_mp[g] = p.op("gpsimd",
                                    lambda e, e13=e13, mp3=mp3, gm3=gm3:
                                    e.tensor_tensor(mp3, e13, gm3, ALU.mult),
                                    (t_e1, t_gmask, T.get("mp_free")))
                                t_mp_prevgrp = t_mp[g]
                                t_relu_prev = t_rl
                            T["rhs1_free"] = t_relu_prev
                            T["nab_free"] = t_mm2_last
                            T["e1_free"] = t_mp[1]
                            dt3 = dslice.unsqueeze(2).broadcast_to(
                                (128, NH, HD))
                            xs3 = xs_c.ap().rearrange("p (h d) -> p h d", d=HD)
                            xd3 = xdt.ap().rearrange("p (h d) -> p h d", d=HD)
                            t_xd = p.op("vector", lambda e, xs3=xs3, xd3=xd3,
                                        dt3=dt3: e.tensor_tensor(
                                            xd3, xs3, dt3, ALU.mult),
                                        (t_xsl[i], t_dt, T.get("xd_free")))
                            ds3 = dsT.ap().unsqueeze(2).broadcast_to(
                                (128, NH, HD))
                            xdd3 = xdd.ap().rearrange("p (h d) -> p h d", d=HD)
                            p.nop("vector")
                            t_xdd = p.op("vector", lambda e, xd3=xd3, xdd3=xdd3,
                                         ds3=ds3: e.tensor_tensor(
                                             xdd3, xd3, ds3, ALU.mult),
                                         (t_xd, t_dsT, T.get("xdd_free")))
                            d3 = d_rep.ap().unsqueeze(2).broadcast_to(
                                (128, NH, HD))
                            t2_3 = t2b.ap().rearrange("p (h d) -> p h d", d=HD)
                            t_t2 = p.op("gpsimd", lambda e, xs3=xs3, t2_3=t2_3,
                                        d3=d3: e.tensor_tensor(
                                            t2_3, xs3, d3, ALU.mult),
                                        (t_xsl[i], T["d_rep"],
                                         T.get("t2_free")))
                            cons_xs[i] = t_t2
                            t_bt_mm = p.op("tensor", lambda e, b=b, ccol=ccol:
                                           e.transpose(psBb[:, 512:640],
                                                       bt_t[b][:, ccol],
                                                       ident[:, :]),
                                           (t_id, t_dsT))
                            t_bc = p.op("vector", lambda e: e.tensor_copy(
                                bct[:, :], psBb[:, 512:640]),
                                (t_bt_mm, T.get("bc_free")))
                            t_spb = p.op("vector", lambda e, spP=spP:
                                         e.tensor_copy(spb[:, :], spP[:, :]),
                                         (T["sp_prev_tok"],
                                          T.get("spb_free")))
                            smts = []
                            for j in range(4):
                                smts.append(p.op("tensor",
                                    lambda e, j=j: e.matmul(
                                        psA[:, j * 512:(j + 1) * 512], bct[:, :],
                                        xdd[:, j * 512:(j + 1) * 512],
                                        start=True, stop=True),
                                    (t_bc, t_xdd, t_relu_prev)))
                            T["bc_free"] = smts[-1]
                            ed3 = edr.ap().unsqueeze(2).broadcast_to(
                                (128, NH, HD))
                            spP3 = spP.ap().rearrange("p (h d) -> p h d", d=HD)
                            spN3 = spN.ap().rearrange("p (h d) -> p h d", d=HD)
                            t_su1 = p.op("vector", lambda e, spN3=spN3,
                                         spP3=spP3, ed3=ed3: e.tensor_tensor(
                                             spN3, spP3, ed3, ALU.mult),
                                         (T["sp_prev_tok"], t_edr,
                                          T.get("yoff_done")))
                            p.nop("vector")
                            t_su2 = p.op("vector", lambda e, spN=spN:
                                         e.tensor_tensor(
                                             spN[:, :], spN[:, :], psA[:, :],
                                             ALU.add), (t_su1, smts[-1]))
                            ymts = []
                            for h in range(NH):
                                ymts.append(p.op("tensor",
                                    lambda e, h=h: e.matmul(
                                        psB[:, h * HD:(h + 1) * HD],
                                        mp[:, h * 128:(h + 1) * 128],
                                        xdt[:, h * HD:(h + 1) * HD],
                                        start=(h % 8 == 0), stop=(h % 8 == 7)),
                                    (t_mp[h // 16], t_xd, t_bc, t_edr,
                                     t_gev, t_dsT, T.get("y_free"))))
                            T["mp_free"] = ymts[-1]
                            T["xd_free"] = ymts[-1]
                            omts = []
                            for j in range(4):
                                omts.append(p.op("tensor",
                                    lambda e, j=j, b=b, ccol=ccol: e.matmul(
                                        psA[:, j * 512:(j + 1) * 512],
                                        ct_t[b][:, ccol],
                                        spb[:, j * 512:(j + 1) * 512],
                                        start=True, stop=True),
                                    (t_spb, t_su2)))
                            T["sp_prev_tok"] = t_su2
                            T["yoff_done"] = omts[-1]
                            T["spb_free"] = omts[-1]
                            T["xdd_free"] = t_su2
                            eat3 = eat.ap().unsqueeze(2).broadcast_to(
                                (128, NH, HD))
                            psA3 = psA.ap().rearrange("p (h d) -> p h d", d=HD)
                            t1_3 = t1f.ap().rearrange("p (h d) -> p h d", d=HD)
                            t_t1f = p.op("vector", lambda e, t1_3=t1_3,
                                         psA3=psA3, eat3=eat3: e.tensor_tensor(
                                             t1_3, psA3, eat3, ALU.mult),
                                         (omts[-1], t_eat))
                            T["psA"] = t_t1f
                            T["eat_free"] = t_t1f
                            p.nop("vector")
                            t_y1 = p.op("vector", lambda e: e.tensor_tensor(
                                y1f[:, :], psB[:, :], t1f[:, :], ALU.add),
                                (ymts[-1], t_t1f))
                            T["psB"] = t_y1
                            T["y_free"] = t_y1
                            T["acs_free"] = t_y1
                            T["acsT_free"] = t_y1
                            p.nop("vector")
                            t_y2 = p.op("vector", lambda e: e.tensor_tensor(
                                y2f[:, :], y1f[:, :], t2b[:, :], ALU.add),
                                (t_y1, t_t2))
                            T["t2_free"] = t_y2
                            p.nop("vector")
                            t_yg = p.op("vector", lambda e, sz_c=sz_c:
                                        e.tensor_tensor(ygf[:, :], y2f[:, :],
                                                        sz_c[:, :], ALU.mult),
                                        (t_y2, t_szl[i]))
                            cons_sz[i] = t_yg
                            t_sq3 = p.op("scalar", lambda e: e.activation(
                                rlt[:, :], ygf[:, :], AF.Square,
                                accum_out=st["ssq3"][:, :]), (t_yg,))
                            t_l3 = p.op("scalar", lambda e: e.activation(
                                st["l3"][:, :], st["ssq3"][:, :], AF.Ln,
                                scale=1.0 / D_INNER, bias=EPS), (t_sq3,))
                            t_r3 = p.op("scalar", lambda e: e.activation(
                                st["rstd3"][:, :], st["l3"][:, :], AF.Exp,
                                scale=-0.5), (t_l3,))
                            p.nop("vector")
                            t_yb = p.op("vector", lambda e: e.tensor_scalar(
                                ybt[:, :], ygf[:, :], st["rstd3"][:, :], None,
                                ALU.mult), (t_r3, t_yg, T.get("ybt_free")))
                            r0 = b * L + c * 128
                            T["ybt_free"] = p.dma(
                                "gpsimd", yb_sp[r0:r0 + 128, :], ybt[:, :],
                                (t_yb,), stream="ybt")
                            emit_qunit()
                        T["sp_free"] = T["sp_prev_tok"]
                    while qunits:
                        emit_qunit()
                p.barrier()
                qstk.close()

            if "p3b" in phases:
                import contextlib as _cl
                with _cl.ExitStack() as ph:
                    g_ = alloc_group(ph, [
                        ("woT", (128, 16 * D_MODEL), BF16),
                        ("ybT0", (128, 16 * 512), BF16),
                        ("ybT1", (128, 16 * 512), BF16),
                        ("xng0", (128, 4 * D_MODEL), F32),
                        ("xng1", (128, 4 * D_MODEL), F32),
                        ("sstg", (128, D_MODEL), F32)])
                    woT = g_["woT"]; ybTs = [g_["ybT0"], g_["ybT1"]]
                    xngs = [g_["xng0"], g_["xng1"]]; sstg = g_["sstg"]
                    t_wo = []
                    for k in range(16):
                        t_wo.append(p.dmat(woT[:, k * D_MODEL:(k + 1) * D_MODEL],
                                           wout_bf[:, k * 128:(k + 1) * 128],
                                           stream="woT"))
                    t_wo = [t_wo[-1]] * 16
                    for b in range(NB):
                        for g in range(4):
                            gi = (b * 4 + g) % 2
                            ybT = ybTs[gi]
                            xng = xngs[gi]
                            r0 = b * L + g * 512
                            t_yl = []
                            for k in range(16):
                                t_yl.append(p.dmat(
                                    ybT[:, k * 512:(k + 1) * 512],
                                    yb_sp[r0:r0 + 512, k * 128:(k + 1) * 128],
                                    (T.get(f"ybT_free{gi}"),),
                                    stream=f"ybT{gi}"))
                            t_yl = [t_yl[-1]] * 16
                            t_xl = []
                            for j4 in range(4):
                                t_xl.append(p.dma(
                                    "sync",
                                    xng[:, j4 * D_MODEL:(j4 + 1) * D_MODEL],
                                    xn_sp[r0 + j4 * 128:r0 + (j4 + 1) * 128, :],
                                    (T.get(f"ybT_free{gi}"),),
                                    stream=f"xng{gi}"))
                            t_xl = [t_xl[-1]] * 4
                            t_s_last = None
                            for j4 in range(4):
                                ps = (psA if j4 < 2 else psB)
                                off = (j4 % 2) * 1024
                                psfree = T.get("psA") if j4 < 2 else T.get("psB")
                                mts = []
                                for k in range(16):
                                    for n2 in range(2):
                                        mts.append(p.op("tensor",
                                            lambda e, ps=ps, off=off, k=k,
                                            j4=j4, n2=n2, ybT=ybT: e.matmul(
                                                ps[:, off + n2 * 512:
                                                   off + (n2 + 1) * 512],
                                                ybT[:, k * 512 + j4 * 128:
                                                    k * 512 + (j4 + 1) * 128],
                                                woT[:, k * D_MODEL + n2 * 512:
                                                    k * D_MODEL +
                                                    (n2 + 1) * 512],
                                                start=(k == 0), stop=(k == 15)),
                                            (t_yl[k], t_wo[k], psfree)))
                                t_s = p.op("vector", lambda e, ps=ps, off=off,
                                           j4=j4, xng=xng: e.tensor_tensor(
                                               sstg[:, :],
                                               ps[:, off:off + 1024],
                                               xng[:, j4 * D_MODEL:
                                                   (j4 + 1) * D_MODEL],
                                               ALU.add),
                                           (mts[-1], t_xl[j4],
                                            T.get("sstg_free")))
                                if j4 == 1:
                                    T["psA"] = t_s
                                if j4 == 3:
                                    T["psB"] = t_s
                                t_s_last = t_s
                                T["sstg_free"] = p.dma(
                                    "gpsimd",
                                    s_sp[r0 + j4 * 128:r0 + (j4 + 1) * 128, :],
                                    sstg[:, :], (t_s,), stream="sstg")
                            T[f"ybT_free{gi}"] = t_s_last
                p.barrier()

            if "p4a" in phases:
                import contextlib as _cl
                with _cl.ExitStack() as ph:
                    g_ = alloc_group(ph, [
                        ("xqT", (128, NK * L), BF16),
                        ("qgT", (128, NK * DFF), BF16),
                        ("quT", (128, NK * DFF), BF16),
                        ("hch0", (128, 960), F32), ("sgch0", (128, 960), BF16),
                        ("hch1", (128, 960), F32), ("sgch1", (128, 960), BF16),
                        ("hqt", (128, DFF), BF16)])
                    xqT = g_["xqT"]; qgT = g_["qgT"]; quT = g_["quT"]
                    hchs = [g_["hch0"], g_["hch1"]]
                    sgchs = [g_["sgch0"], g_["sgch1"]]
                    hqt = g_["hqt"]
                    q4sl = [alloc_qscr(ph, "q4A", 960),
                            alloc_qscr(ph, "q4B", 960)]
                    q4i = {"i": 0}
                    t_qg = []
                    t_qu = []
                    for k in range(NK):
                        t_qg.append(p.dmat(qgT[:, k * DFF:(k + 1) * DFF],
                                           qg_bf[:, k * 128:(k + 1) * 128],
                                           stream="qgu"))
                        t_qu.append(p.dmat(quT[:, k * DFF:(k + 1) * DFF],
                                           qu_bf[:, k * 128:(k + 1) * 128],
                                           stream="qgu"))
                    t_qg = [t_qu[-1]] * NK
                    t_qu = [t_qu[-1]] * NK
                    chunks = [(0, 960), (960, 960), (1920, DFF - 1920)]
                    for b in range(NB):
                        t_xq = []
                        for k in range(NK):
                            t_xq.append(p.dmat(
                                xqT[:, k * L:(k + 1) * L],
                                xq_sp[b * L:(b + 1) * L, k * 128:(k + 1) * 128],
                                (T.get("xqT_free"),), stream="xqT"))
                        t_xq = [t_xq[-1]] * NK
                        for t in range(NT):
                            r0 = b * L + t * 128
                            qtoks = []
                            for ci, (f0, fw) in enumerate(chunks):
                                pi = q4i["i"] % 2
                                sl2 = q4i["i"] * 2
                                q4i["i"] += 1
                                gslot = sl2 % 4
                                uslot = (sl2 + 1) % 4
                                pg = (psA if gslot < 2 else psB)[
                                    :, (gslot % 2) * 1024:
                                       (gslot % 2) * 1024 + fw]
                                pu = (psA if uslot < 2 else psB)[
                                    :, (uslot % 2) * 1024:
                                       (uslot % 2) * 1024 + fw]
                                depg = T.get(f"p4g{gslot}",
                                             T.get("psA") if gslot < 2
                                             else T.get("psB"))
                                depu = T.get(f"p4g{uslot}",
                                             T.get("psA") if uslot < 2
                                             else T.get("psB"))
                                gmts, umts = [], []
                                for h0 in range(0, fw, 512):
                                    mw = min(512, fw - h0)
                                    for k in range(NK):
                                        gmts.append(p.op("tensor",
                                            lambda e, pg=pg, k=k, t=t, f0=f0,
                                            h0=h0, mw=mw: e.matmul(
                                                pg[:, h0:h0 + mw],
                                                xqT[:, k * L + t * 128:
                                                    k * L + (t + 1) * 128],
                                                qgT[:, k * DFF + f0 + h0:
                                                    k * DFF + f0 + h0 + mw],
                                                start=(k == 0),
                                                stop=(k == NK - 1)),
                                            (t_xq[k], t_qg[k], depg)))
                                for h0 in range(0, fw, 512):
                                    mw = min(512, fw - h0)
                                    for k in range(NK):
                                        umts.append(p.op("tensor",
                                            lambda e, pu=pu, k=k, t=t, f0=f0,
                                            h0=h0, mw=mw: e.matmul(
                                                pu[:, h0:h0 + mw],
                                                xqT[:, k * L + t * 128:
                                                    k * L + (t + 1) * 128],
                                                quT[:, k * DFF + f0 + h0:
                                                    k * DFF + f0 + h0 + mw],
                                                start=(k == 0),
                                                stop=(k == NK - 1)),
                                            (t_xq[k], t_qu[k], depu)))
                                hch = hchs[pi]
                                sgch = sgchs[pi]
                                t_sg = p.op("scalar", lambda e, pg=pg, fw=fw,
                                            sgch=sgch:
                                            e.activation(sgch[:, :fw], pg,
                                                         AF.Silu),
                                            (gmts[-1],
                                             T.get(f"sgch_free{pi}")))
                                t_h = p.op("vector", lambda e, pu=pu, fw=fw,
                                           hch=hch, sgch=sgch:
                                           e.tensor_tensor(
                                               hch[:, :fw], sgch[:, :fw], pu,
                                               ALU.mult),
                                           (t_sg, umts[-1],
                                            T.get(f"hch_free{pi}")))
                                t_q = quant(hch[:, :fw], hqt[:, f0:f0 + fw],
                                            fw, q4sl[pi](fw),
                                            (t_h, T.get("hqt_free")),
                                            chain=f"h{pi}")
                                qtoks.append(t_q)
                                T[f"p4g{gslot}"] = t_sg
                                T[f"p4g{uslot}"] = t_h
                                T[f"sgch_free{pi}"] = t_h
                                T[f"hch_free{pi}"] = t_q
                            T["hqt_free"] = p.dma(
                                "gpsimd", hq_sp[r0:r0 + 128, :], hqt[:, :],
                                tuple(qtoks), stream="hqt")
                        T["xqT_free"] = T["hqt_free"]
                p.barrier()

            if "p4b" in phases:
                import contextlib as _cl
                with _cl.ExitStack() as ph:
                    g_ = alloc_group(ph, [
                        ("qdT", (128, 22 * D_MODEL), BF16),
                        ("hqT0", (128, 22 * 512), BF16),
                        ("hqT1", (128, 22 * 512), BF16),
                        ("sg40", (128, 4 * D_MODEL), F32),
                        ("sg41", (128, 4 * D_MODEL), F32),
                        ("xg40", (128, 4 * D_MODEL), F32),
                        ("xg41", (128, 4 * D_MODEL), F32),
                        ("gti", (128, D_MODEL), F32),
                        ("fstg", (128, D_MODEL), F32)])
                    qdT = g_["qdT"]; hqTs = [g_["hqT0"], g_["hqT1"]]
                    sg4s = [g_["sg40"], g_["sg41"]]
                    xg4s = [g_["xg40"], g_["xg41"]]
                    gti = g_["gti"]; fstg = g_["fstg"]
                    t_qd = []
                    for k in range(22):
                        t_qd.append(p.dmat(qdT[:, k * D_MODEL:(k + 1) * D_MODEL],
                                           qd_bf[:, k * 128:(k + 1) * 128],
                                           stream="qdT"))
                    t_qd = [t_qd[-1]] * 22
                    for b in range(NB):
                        for g in range(4):
                            gi = (b * 4 + g) % 2
                            hqT = hqTs[gi]
                            sg4 = sg4s[gi]
                            xg4 = xg4s[gi]
                            r0 = b * L + g * 512
                            t_hl = []
                            for k in range(22):
                                t_hl.append(p.dmat(
                                    hqT[:, k * 512:(k + 1) * 512],
                                    hq_sp[r0:r0 + 512, k * 128:(k + 1) * 128],
                                    (T.get(f"hqT_free{gi}"),),
                                    stream=f"hqT{gi}"))
                            t_hl = [t_hl[-1]] * 22
                            t_sl = []
                            t_xl = []
                            for j4 in range(4):
                                t_sl.append(p.dma(
                                    "sync",
                                    sg4[:, j4 * D_MODEL:(j4 + 1) * D_MODEL],
                                    s_sp[r0 + j4 * 128:r0 + (j4 + 1) * 128, :],
                                    (T.get(f"hqT_free{gi}"),),
                                    stream=f"sgx{gi}"))
                                t_xl.append(p.dma(
                                    "sync",
                                    xg4[:, j4 * D_MODEL:(j4 + 1) * D_MODEL],
                                    x_in[b, g * 512 + j4 * 128:
                                         g * 512 + (j4 + 1) * 128, :],
                                    (T.get(f"hqT_free{gi}"),),
                                    stream=f"sgx{gi}"))
                            t_sl = [t_xl[-1]] * 4
                            t_xl = [t_xl[-1]] * 4
                            t_g_sc = [None, None]
                            for j4 in range(4):
                                ps = (psA if j4 < 2 else psB)
                                off = (j4 % 2) * 1024
                                psfree = T.get("psA") if j4 < 2 else T.get("psB")
                                mts = []
                                for k in range(22):
                                    for n2 in range(2):
                                        mts.append(p.op("tensor",
                                            lambda e, ps=ps, off=off, k=k,
                                            n2=n2, j4=j4, hqT=hqT: e.matmul(
                                                ps[:, off + n2 * 512:
                                                   off + (n2 + 1) * 512],
                                                hqT[:, k * 512 + j4 * 128:
                                                    k * 512 + (j4 + 1) * 128],
                                                qdT[:, k * D_MODEL + n2 * 512:
                                                    k * D_MODEL +
                                                    (n2 + 1) * 512],
                                                start=(k == 0),
                                                stop=(k == 21)),
                                            (t_hl[k], t_qd[k],
                                             psfree if k == 0 else None)))
                                t_g = p.op("scalar", lambda e, ps=ps, off=off:
                                           e.activation(
                                               gti[:, :], ps[:, off:off + 1024],
                                               AF.Silu),
                                           (mts[-1], T.get("gti_free")))
                                if j4 == 1:
                                    T["psA"] = t_g
                                if j4 == 3:
                                    T["psB"] = t_g
                                if g_dbg is not None:
                                    p.dma("gpsimd",
                                          g_dbg[r0 + j4 * 128:
                                                r0 + (j4 + 1) * 128, :],
                                          gti[:, :], (t_g,), stream="dbg")
                                t_f1 = p.op("vector", lambda e, j4=j4, sg4=sg4:
                                            e.tensor_tensor(
                                                fstg[:, :],
                                                sg4[:, j4 * D_MODEL:
                                                    (j4 + 1) * D_MODEL],
                                                gti[:, :], ALU.mult),
                                            (t_g, t_sl[j4],
                                             T.get("fstg_free")))
                                T["gti_free"] = t_f1
                                p.nop("vector")
                                t_f2 = p.op("vector", lambda e, j4=j4, xg4=xg4:
                                            e.tensor_tensor(
                                                fstg[:, :], fstg[:, :],
                                                xg4[:, j4 * D_MODEL:
                                                    (j4 + 1) * D_MODEL],
                                                ALU.add), (t_f1, t_xl[j4]))
                                T["fstg_free"] = p.dma(
                                    "gpsimd",
                                    out[b, g * 512 + j4 * 128:
                                        g * 512 + (j4 + 1) * 128, :],
                                    fstg[:, :], (t_f2,), stream="fstg")
                            T[f"hqT_free{gi}"] = T["fstg_free"]
                p.barrier()

        p.finalize(block)
    return nc


def _make_in_maps(inputs):
    x = np.ascontiguousarray(inputs["L0_attn_x"])
    common = {
        "norm_w": np.ascontiguousarray(inputs["L0_attn_norm_w"]),
        "in_proj_w": np.ascontiguousarray(inputs["L0_attn_in_proj_w"]),
        "conv_w": np.ascontiguousarray(inputs["L0_attn_conv_w"]),
        "conv_b": np.ascontiguousarray(inputs["L0_attn_conv_b"]),
        "dt_bias": np.ascontiguousarray(inputs["L0_attn_dt_bias"]),
        "A_log": np.ascontiguousarray(inputs["L0_attn_A_log"]),
        "D": np.ascontiguousarray(inputs["L0_attn_D"]),
        "gnorm_w": np.ascontiguousarray(inputs["L0_attn_gnorm_w"]),
        "out_proj_w": np.ascontiguousarray(inputs["L0_attn_out_proj_w"]),
        "ffn_w_gate": np.ascontiguousarray(inputs["L0_ffn_w_gate"]),
        "ffn_w_up": np.ascontiguousarray(inputs["L0_ffn_w_up"]),
        "ffn_w_down": np.ascontiguousarray(inputs["L0_ffn_w_down"]),
    }
    in_maps = []
    for i in range(8):
        m = dict(common)
        m["x"] = np.ascontiguousarray(x[2 * i:2 * i + 2])
        in_maps.append(m)
    return in_maps


def kernel(**inputs):
    nc = bass.Bass("TRN2", target_bir_lowering=False)
    build(nc)
    in_maps = _make_in_maps(inputs)
    res = run_bass_kernel_spmd(nc, in_maps, core_ids=list(range(8)))
    return np.concatenate([r["out"] for r in res.results],
                          axis=0).astype(np.float32)


# version 17 (final)
# speedup vs baseline: 1.1567x; 1.1567x over previous
import numpy as np
import ml_dtypes
import concourse.bass as bass
import concourse.mybir as mybir
from concourse.bass_utils import run_bass_kernel_spmd

F32 = mybir.dt.float32
BF16 = mybir.dt.bfloat16
I32 = mybir.dt.int32
AF = mybir.ActivationFunctionType
ALU = mybir.AluOpType
AX = mybir.AxisListType

ENGS = ("sync", "gpsimd", "vector", "scalar", "tensor")

D_MODEL = 1024
D_INNER = 2048
NH = 32
HD = 64
CONV_DIM = 2304
D_IN_PROJ = 4384
WPAD = 4384
DFF = 2816
L = 2048
NB = 2
NT = 16
NK = 8
NC18 = 18
EPS = 1e-5

MSK_RND = 0x200000
MSK_AND = 0xFFC00000 - (1 << 32)

ALL_PHASES = ("p0", "p1", "p2", "p3", "p3b", "p4a", "p4b")


class P:

    def __init__(self, nc, sems, mksem):
        self.nc = nc
        self.sems = dict(sems)
        self.mksem = mksem
        self.q = {e: [] for e in ENGS}
        self.cnt = {k: 0 for k in ENGS}
        self.wm = {e: {} for e in ENGS}

    def stream(self, name):
        key = "D:" + name
        if key not in self.sems:
            self.sems[key] = self.mksem(key)
            self.cnt[key] = 0
        return key

    def _wait(self, eng, key, v):
        if v <= 0:
            return
        if key == eng:
            pass
        if v > self.wm[eng].get(key, 0):
            assert v <= self.cnt[key], f"wait-ahead {eng}<-{key}: {v}>{self.cnt[key]}"
            self.wm[eng][key] = v
            s = self.sems[key]
            self.q[eng].append(lambda e, s=s, v=v: e.wait_ge(s, v))

    def op(self, eng, fn, deps=(), inc=1, produce=True):
        for d in deps:
            if not d:
                continue
            self._wait(eng, d[0], d[1])
        if produce:
            self.cnt[eng] += inc
            s = self.sems[eng]
            self.q[eng].append(
                lambda e, fn=fn, s=s, i=inc: fn(e).then_inc(s, i))
            return (eng, self.cnt[eng])
        self.q[eng].append(lambda e, fn=fn: fn(e))
        return (eng, self.cnt[eng])

    def dma(self, eng, dst, src, deps=(), stream=None):
        assert stream is not None
        key = self.stream(stream)
        for d in deps:
            if not d:
                continue
            self._wait(eng, d[0], d[1])
        self.cnt[key] += 16
        s = self.sems[key]
        self.q[eng].append(
            lambda e, s=s, dst=dst, src=src: e.dma_start(
                out=dst, in_=src).then_inc(s, 16))
        return (key, self.cnt[key])

    def dmat(self, dst, src, deps=(), stream=None):
        assert stream is not None
        key = self.stream(stream)
        for d in deps:
            if not d:
                continue
            self._wait("sync", d[0], d[1])
        self.cnt[key] += 16
        s = self.sems[key]
        self.q["sync"].append(
            lambda e, s=s, dst=dst, src=src: e.dma_start_transpose(
                dst, src).then_inc(s, 16))
        return (key, self.cnt[key])

    def nop(self, eng):
        self.op(eng, lambda e: e.engine_nop(), (), produce=False)

    def barrier(self):
        toks = [(k, self.cnt[k]) for k in self.cnt]
        for e in ENGS:
            for k, v in toks:
                if k == e:
                    continue
                self._wait(e, k, v)

    def finalize(self, block):
        @block.sync
        def _(e):
            for f in self.q["sync"]:
                f(e)

        @block.gpsimd
        def _(e):
            for f in self.q["gpsimd"]:
                f(e)

        @block.vector
        def _(e):
            for f in self.q["vector"]:
                f(e)

        @block.scalar
        def _(e):
            for f in self.q["scalar"]:
                f(e)

        @block.tensor
        def _(e):
            for f in self.q["tensor"]:
                f(e)


def cdiv(a, b):
    return (a + b - 1) // b


def build(nc, dbg=(), phases=ALL_PHASES):
    import contextlib
    dbg = set(dbg)
    phases = set(phases)

    def din(name, shape):
        return nc.dram_tensor(name, list(shape), F32, kind="ExternalInput").ap()

    x_in = din("x", (NB, L, D_MODEL))
    w_norm = din("norm_w", (D_MODEL,))
    w_inp = din("in_proj_w", (D_IN_PROJ, D_MODEL))
    w_conv = din("conv_w", (CONV_DIM, 1, 4))
    b_conv = din("conv_b", (CONV_DIM,))
    dt_bias = din("dt_bias", (NH,))
    a_log = din("A_log", (NH,))
    d_in = din("D", (NH,))
    w_gnorm = din("gnorm_w", (D_INNER,))
    w_out = din("out_proj_w", (D_MODEL, D_INNER))
    w_gate = din("ffn_w_gate", (DFF, D_MODEL))
    w_up = din("ffn_w_up", (DFF, D_MODEL))
    w_down = din("ffn_w_down", (D_MODEL, DFF))

    def scr_dram(name, shape, dtype, dbgkey=None, shared=False):
        kind = "ExternalOutput" if (dbgkey and dbgkey in dbg) else "Internal"
        kw = {}
        return nc.dram_tensor(name, list(shape), dtype, kind=kind, **kw).ap()

    winp_bf = scr_dram("winp_bf", (D_IN_PROJ, D_MODEL), BF16, "winp")
    wout_bf = scr_dram("wout_bf", (D_MODEL, D_INNER), BF16, "wout")
    qg_bf = scr_dram("qg_bf", (DFF, D_MODEL), BF16, "qg", shared=True)
    qu_bf = scr_dram("qu_bf", (DFF, D_MODEL), BF16, "qu", shared=True)
    qd_bf = scr_dram("qd_bf", (D_MODEL, DFF), BF16, "qd", shared=True)
    xn_sp = scr_dram("xn_sp", (NB * L, D_MODEL), F32, "xn")
    xdbg = scr_dram("xdbg", (NB * L, D_MODEL), F32, "p1dbg") if "p1dbg" in dbg else None
    sdbg = scr_dram("sdbg", (NB * NT, 128, 8), F32, "p1dbg") if "p1dbg" in dbg else None
    u_sp = scr_dram("u_sp", (NB * L, D_MODEL), BF16, "u")
    xq_sp = scr_dram("xq_sp", (NB * L, D_MODEL), BF16, "xq")
    xs_sp = scr_dram("xs_sp", (NB, D_INNER, L), BF16, "xs")
    sz_sp = scr_dram("sz_sp", (NB * L, D_INNER), BF16, "sz")
    dt_dbg = scr_dram("dt_dbg", (NB, 128, NT * NH), F32, "dt") if "dt" in dbg else None
    yb_sp = scr_dram("yb_sp", (NB * L, D_INNER), BF16, "yb")
    s_sp = scr_dram("s_sp", (NB * L, D_MODEL), F32, "s")
    hq_sp = scr_dram("hq_sp", (NB * L, DFF), BF16, "hq")
    g_dbg = scr_dram("g_dbg", (NB * L, D_MODEL), F32, "g") if "g" in dbg else None
    out = nc.dram_tensor("out", [NB, L, D_MODEL], F32, kind="ExternalOutput").ap()

    sb = nc.alloc_sbuf_tensor
    ident = sb("ident", [128, 128], BF16)
    u1f = sb("u1f", [128, 128], F32)
    u1b = sb("u1b", [128, 128], BF16)
    ones_b = sb("ones_b", [128, 128], BF16)
    ones_f = sb("ones_f", [128, 128], F32)
    dtb_rep = sb("dtb_rep", [128, NH], F32)
    a_rep = sb("a_rep", [128, NH], F32)
    d_rep = sb("d_rep", [128, NH], BF16)
    convw = sb("convw", [128, NC18 * 4], F32)
    convb = sb("convb", [128, NC18], F32)
    bt_t = [sb(f"bt{b}", [128, L], BF16) for b in range(NB)]
    ct_t = [sb(f"ct{b}", [128, L], BF16) for b in range(NB)]
    dt_sp = [sb(f"dtsp{b}", [128, NT * NH], F32) for b in range(NB)]
    st = {n: sb(f"st_{n}", [128, 1], F32)
          for n in ("xsum", "m", "mm", "va", "ssq", "rstd", "ssq2", "rstd2",
                    "ssq3", "l3", "rstd3")}
    junkq = sb("junkq", [128, D_MODEL], BF16)
    dbg8 = sb("dbg8", [128, 8], F32)
    p1xa = sb("p1xa", [128, D_MODEL], F32)
    p1xb = sb("p1xb", [128, D_MODEL], F32)
    p1xnt = sb("p1xnt", [128, D_MODEL], F32)
    p1ut = sb("p1ut", [128, D_MODEL], BF16)

    psA = nc.alloc_psum_tensor("psA", [128, 2048], F32)
    psB = nc.alloc_psum_tensor("psB", [128, 2048], F32)
    psBb = psB.ap().bitcast(BF16)

    with contextlib.ExitStack() as stk, nc.Block() as block:
        sems = {e: stk.enter_context(nc.semaphore(f"sem_{e}")) for e in ENGS}
        p = P(nc, sems, lambda nm: stk.enter_context(nc.semaphore(nm)))
        T = {}

        def alloc_group(stk2, specs):
            return {name: stk2.enter_context(
                nc.sbuf_tensor(name, list(shape), dtype))
                for name, shape, dtype in specs}

        def quant(sx, dst, W, scr, deps, chain="g"):
            ck = "qscr_last_" + chain
            deps = tuple(deps) + (T.get(ck),)
            nb16 = W // 16
            red, red6, rcl = scr["red"], scr["red6"], scr["rcl"]
            ax, v, t1i, i1 = scr["ax"], scr["v"], scr["t1i"], scr["i1"]
            ap_, qa, sgn = scr["ap"], scr["qa"], scr["sgn"]
            rows = sx.shape[0]
            x3 = sx.rearrange("p (a b) -> p a b", b=16)
            p.nop("vector")
            t_red = p.op("vector", lambda e: e.reduce_max(
                red, x3, AX.X, apply_absolute_value=True), deps)
            p.nop("vector")
            t_red6 = p.op("vector", lambda e: e.tensor_scalar(
                red6, red, 6e-12, 1.0 / 6.0, ALU.max, ALU.mult), (t_red,))
            p.nop("vector")
            t_rcl = p.op("vector", lambda e: e.reciprocal(rcl, red6), (t_red6,))
            t_ax = p.op("scalar", lambda e: e.activation(ax, sx, AF.Abs), deps)
            t_sgn = p.op("scalar", lambda e: e.activation(sgn, sx, AF.Sign), deps)
            rb = rcl.unsqueeze(2).broadcast_to((rows, nb16, 16))
            a3 = ax.rearrange("p (a b) -> p a b", b=16)
            v3 = v.rearrange("p (a b) -> p a b", b=16)
            t_v = p.op("gpsimd", lambda e: e.tensor_tensor(v3, a3, rb, ALU.mult),
                       (t_ax, t_rcl))
            t_t1 = p.op("vector", lambda e: e.tensor_scalar(
                t1i, v, 2.0, None, ALU.mult), (t_v,))
            t_i1a = p.op("vector", lambda e: e.tensor_scalar(
                i1, v.bitcast(I32), MSK_RND, None, ALU.add), (t_v,))
            t_ap = p.op("vector", lambda e: e.tensor_scalar(
                ap_, t1i, 4.0, 0.5, ALU.min, ALU.mult), (t_t1,))
            t_i1b = p.op("vector", lambda e: e.tensor_scalar(
                i1, i1, MSK_AND, None, ALU.bitwise_and), (t_i1a,))
            p.nop("vector")
            t_bm = p.op("vector", lambda e: e.scalar_tensor_tensor(
                v, i1.bitcast(F32), 3.0, i1.bitcast(F32),
                ALU.is_ge, ALU.mult), (t_i1b,))
            t_qa = p.op("vector", lambda e: e.tensor_tensor(
                qa, ap_, v, ALU.max), (t_ap, t_bm))
            p.nop("vector")
            t_qs = p.op("vector", lambda e: e.tensor_tensor(
                qa, qa, sgn, ALU.mult), (t_qa, t_sgn))
            r6b = red6.unsqueeze(2).broadcast_to((rows, nb16, 16))
            q3 = qa.rearrange("p (a b) -> p a b", b=16)
            d3 = dst.rearrange("p (a b) -> p a b", b=16)
            t_qb = p.op("gpsimd", lambda e: e.tensor_tensor(d3, q3, r6b, ALU.mult),
                        (t_qs, t_red6))
            T[ck] = t_qb
            return t_qb

        c_eps = nc.alloc_sbuf_tensor("c_eps", [128, 1], F32)
        c_one = nc.alloc_sbuf_tensor("c_one", [128, 1], F32)
        p.op("gpsimd", lambda e: e.memset(c_eps[:, :], EPS))
        p.op("gpsimd", lambda e: e.memset(c_one[:, :], 1.0))
        nc.const_aps.register(F32, EPS, c_eps.ap())
        nc.const_aps.register(F32, 1.0, c_one.ap())
        t_id0 = p.op("gpsimd", lambda e: e.memset(ident[:, :], 0.0))
        t_id = p.op("gpsimd", lambda e: e.affine_select(
            out=ident[:, :], in_=ident[:, :], compare_op=ALU.not_equal, fill=1.0,
            base=0, pattern=[[-1, 128]], channel_multiplier=1), (t_id0,))
        t_u10 = p.op("gpsimd", lambda e: e.memset(u1f[:, :], 1.0))
        t_u1 = p.op("gpsimd", lambda e: e.affine_select(
            out=u1f[:, :], in_=u1f[:, :], compare_op=ALU.is_ge, fill=0.0,
            base=0, pattern=[[1, 128]], channel_multiplier=-1), (t_u10,))
        t_u1b = p.op("vector", lambda e: e.tensor_copy(u1b[:, :], u1f[:, :]),
                     (t_u1,))
        t_onesb = p.op("gpsimd", lambda e: e.memset(ones_b[:, :], 1.0))
        t_onesf = p.op("gpsimd", lambda e: e.memset(ones_f[:, :], 1.0))

        t_convw = None
        for c in range(NC18):
            t_convw = p.dma(
                "sync", convw.ap().rearrange("p (t k) -> p t k", k=4)[:, c],
                w_conv.rearrange("(t p) o k -> t p (o k)", p=128)[c],
                stream="cwb")
        t_convb = None
        for c in range(NC18):
            t_convb = p.dma("sync", convb[:, c:c + 1],
                            b_conv.rearrange("(t p o) -> t p o", p=128, o=1)[c],
                            stream="cwb")
        t_convw = t_convb

        import contextlib as _cl0
        with _cl0.ExitStack() as ph0:
            qstk = ph0.enter_context(_cl0.ExitStack())
            g0_ = alloc_group(qstk, [
                ("norm_rep", (128, D_MODEL), F32),
                ("gnorm_rep", (128, D_INNER), F32)])
            norm_rep = g0_["norm_rep"]; gnorm_rep = g0_["gnorm_rep"]

            def alloc_qscr(stk_, tag, W):
                g = alloc_group(stk_, [
                    (tag + "red", (128, W // 16), F32),
                    (tag + "red6", (128, W // 16), F32),
                    (tag + "ax", (128, W), F32), (tag + "v", (128, W), F32),
                    (tag + "i1", (128, W), I32),
                    (tag + "ap", (128, W), BF16),
                    (tag + "sgn", (128, W), BF16)])

                def sl(ncols):
                    return dict(
                        red=g[tag + "red"][:, :ncols // 16],
                        red6=g[tag + "red6"][:, :ncols // 16],
                        rcl=g[tag + "red"][:, :ncols // 16],
                        ax=g[tag + "ax"][:, :ncols], v=g[tag + "v"][:, :ncols],
                        t1i=g[tag + "ax"][:, :ncols].bitcast(I32),
                        i1=g[tag + "i1"][:, :ncols],
                        ap=g[tag + "ap"][:, :ncols],
                        qa=g[tag + "i1"][:, :ncols // 2].bitcast(BF16),
                        sgn=g[tag + "sgn"][:, :ncols])
                return sl

            gq_ = alloc_group(qstk, [
                ("qw_w0", (128, 1024), F32), ("qw_b0", (128, 1024), BF16),
                ("qw_w1", (128, 1024), F32), ("qw_b1", (128, 1024), BF16)])
            gqsl = alloc_qscr(qstk, "qwA", 1024)
            qlast = {"w": [None, None], "b": [None, None], "i": 0}
            qunits = []

            def _mk_qunit(srcap, dstap, ncols, extra=None):
                def emit():
                    pi = qlast["i"] % 2
                    qlast["i"] += 1
                    wt = gq_[f"qw_w{pi}"]
                    bt = gq_[f"qw_b{pi}"]
                    tl = p.dma("sync", wt[:, :ncols], srcap,
                               (qlast["w"][pi], extra), stream=f"wt{pi}")
                    tq = quant(wt[:, :ncols], bt[:, :ncols],
                               ncols, gqsl(ncols), (tl, qlast["b"][pi]),
                               chain="w")
                    qlast["w"][pi] = tq
                    qlast["b"][pi] = p.dma("gpsimd", dstap, bt[:, :ncols],
                                           (tq,), stream=f"btile{pi}")
                return emit

            def emit_qunit():
                if qunits:
                    qunits.pop(0)()

            T["psB"] = None

            rbstk = _cl0.ExitStack()
            grb_ = alloc_group(rbstk, [
                ("rowbuf", (1, D_INNER), F32),
                ("dtmp", (128, NH), F32)])
            rowbuf = grb_["rowbuf"]; dtmp = grb_["dtmp"]

            def replicate_row(dst_sb, src_row_ap, width):
                tok_l = p.dma("sync", rowbuf[:, :width], src_row_ap,
                              (T.get("rowbuf_free"),), stream="row")
                mts = []
                for j0 in range(0, width, 512):
                    w = min(512, width - j0)
                    mts.append(p.op("tensor", lambda e, j0=j0, w=w: e.matmul(
                        psB[:, j0:j0 + w], ones_f[0:1, :], rowbuf[:, j0:j0 + w],
                        start=True, stop=True), (tok_l, t_onesf, T["psB"])))
                tok_e = p.op("vector", lambda e: e.tensor_copy(
                    dst_sb[:, :width], psB[:, :width]), tuple(mts))
                T["psB"] = tok_e
                T["rowbuf_free"] = mts[-1]
                return tok_e

            t_nr = replicate_row(norm_rep, w_norm.rearrange("(a b) -> a b", a=1),
                                 D_MODEL)
            t_gr = replicate_row(gnorm_rep,
                                 w_gnorm.rearrange("(a b) -> a b", a=1), D_INNER)
            t_dtb = replicate_row(dtb_rep, dt_bias.rearrange("(a b) -> a b", a=1),
                                  NH)
            t_dr = replicate_row(dtmp, d_in.rearrange("(a b) -> a b", a=1), NH)
            t_drb = p.op("vector", lambda e: e.tensor_copy(
                d_rep[:, :], dtmp[:, :]), (t_dr,))
            t_ar = replicate_row(a_rep, a_log.rearrange("(a b) -> a b", a=1), NH)
            t_ar2 = p.op("scalar", lambda e: e.activation(
                a_rep[:, :], a_rep[:, :], AF.Exp), (t_ar,))
            t_ar3 = p.op("vector", lambda e: e.tensor_scalar(
                a_rep[:, :], a_rep[:, :], -1.0, None, ALU.mult), (t_ar2,))
            T["a_rep"] = t_ar3
            T["dtb_rep"] = t_dtb
            T["d_rep"] = t_drb
            T["rb_done"] = (t_ar3, t_drb)
            rbstk.close()

            if "p0" in phases:
                import contextlib as _cl
                with _cl.ExitStack() as ph:
                    g_ = alloc_group(ph, [
                        ("p0w", (128, DFF), F32), ("p0b", (128, DFF), BF16),
                        ("p0red", (128, DFF // 16), F32),
                        ("p0red6", (128, DFF // 16), F32),
                        ("p0rcl", (128, DFF // 16), F32),
                        ("p0ax", (128, DFF), F32), ("p0v", (128, DFF), F32),
                        ("p0t1i", (128, DFF), I32), ("p0i1", (128, DFF), I32),
                        ("p0ap", (128, DFF), BF16), ("p0msk", (128, DFF), BF16),
                        ("p0qa", (128, DFF), BF16), ("p0sgn", (128, DFF), BF16)])
                    wtiles = [g_["p0w"], g_["p0ax"]]
                    btiles = [g_["p0b"], g_["p0ap"]]
                    last = {"w": list(T["rb_done"]), "b": [None, None], "i": 0}

                    def prep_fold(src, dst, nrows, ncols, rep, rep_tok):
                        for i in range(cdiv(nrows, 128)):
                            pi = last["i"] % 2
                            last["i"] += 1
                            wtile = wtiles[pi]
                            btile = btiles[pi]
                            r0 = i * 128
                            rr = min(128, nrows - r0)
                            tl = p.dma("sync", wtile[0:rr, :ncols],
                                       src[r0:r0 + rr, :], (last["w"][pi],),
                                       stream=f"pf{pi}")
                            tf = p.op("vector", lambda e, rr=rr, ncols=ncols,
                                      wtile=wtile, btile=btile:
                                      e.tensor_tensor(
                                          btile[0:rr, :ncols],
                                          wtile[0:rr, :ncols],
                                          rep[0:rr, :ncols], ALU.mult),
                                      (tl, rep_tok, last["b"][pi]))
                            last["w"][pi] = tf
                            last["b"][pi] = p.dma("gpsimd", dst[r0:r0 + rr, :],
                                                  btile[0:rr, :ncols], (tf,),
                                                  stream=f"pb{pi}")

                    prep_fold(w_inp, winp_bf, D_IN_PROJ, D_MODEL, norm_rep, t_nr)
                    T["winp_done"] = (last["b"][0], last["b"][1])
                    prep_fold(w_out, wout_bf, D_MODEL, D_INNER, gnorm_rep, t_gr)

                    for i in range(22):
                        qunits.append(_mk_qunit(
                            w_gate[i * 128:(i + 1) * 128, :],
                            qg_bf[i * 128:(i + 1) * 128, :], D_MODEL))
                    for i in range(22):
                        qunits.append(_mk_qunit(
                            w_up[i * 128:(i + 1) * 128, :],
                            qu_bf[i * 128:(i + 1) * 128, :], D_MODEL))
                    for i in range(8):
                        for c0, cw in ((0, 1024), (1024, 1024), (2048, 768)):
                            qunits.append(_mk_qunit(
                                w_down[i * 128:(i + 1) * 128, c0:c0 + cw],
                                qd_bf[i * 128:(i + 1) * 128, c0:c0 + cw], cw))
                    T["p0done"] = p.op(
                        "gpsimd", lambda e: e.memset(dbg8[:, 0:1], 0.0),
                        (last["b"][0], last["b"][1]))

            if "p2" in phases:
                import contextlib as _cl
                with _cl.ExitStack() as ph:
                    g_ = alloc_group(ph, [
                        ("uT", (128, NK * L), BF16),
                        ("winpT", (128, NK * WPAD), BF16),
                        ("preact", (128, L + 4), BF16),
                        ("accA", (128, L), BF16), ("accB", (128, L), BF16),
                        ("zst", (128, 512), BF16)])
                    uT = g_["uT"]; wT = g_["winpT"]; pre = g_["preact"]
                    accA = g_["accA"]; accB = g_["accB"]
                    zst = g_["zst"]
                    xbufs = [p1xa, p1xb]
                    xnt = p1xnt; ut = p1ut
                    p1s = {"cons": [None, None], "xn_cons": (None, None)}
                    T.setdefault("pre_free", T.get("p0done"))

                    def emit_p1_tile(i):
                        b, t = divmod(i, NT)
                        xb = xbufs[i % 2]
                        r0 = b * L + t * 128
                        tl = p.dma("sync", xb[:, :],
                                   x_in[b, t * 128:(t + 1) * 128, :],
                                   (p1s["cons"][i % 2],), stream=f"x{i % 2}")
                        t_sum = p.op("vector", lambda e, xb=xb: e.reduce_sum(
                            st["xsum"][:, :], xb[:, :], AX.X), (tl,))
                        t_sq = p.op("scalar", lambda e, xb=xb: e.activation(
                            junkq[:, :D_MODEL], xb[:, :], AF.Square,
                            accum_out=st["ssq"][:, :]), (tl,))
                        p.nop("vector")
                        t_m = p.op("vector", lambda e: e.tensor_scalar(
                            st["m"][:, :], st["xsum"][:, :], 1.0 / D_MODEL,
                            None, ALU.mult), (t_sum,))
                        p.nop("vector")
                        t_mm = p.op("vector", lambda e: e.tensor_tensor(
                            st["mm"][:, :], st["m"][:, :], st["m"][:, :],
                            ALU.mult), (t_m,))
                        p.nop("vector")
                        t_va = p.op("vector", lambda e: e.scalar_tensor_tensor(
                            st["va"][:, :], st["ssq"][:, :], 1.0 / D_MODEL,
                            st["mm"][:, :], ALU.mult, ALU.subtract),
                            (t_sq, t_mm))
                        t_l = p.op("scalar", lambda e: e.activation(
                            st["va"][:, :], st["va"][:, :], AF.Ln, bias=EPS),
                            (t_va,))
                        t_rstd = p.op("scalar", lambda e: e.activation(
                            st["rstd"][:, :], st["va"][:, :], AF.Exp,
                            scale=-0.5), (t_l,))
                        t_xn = p.op("vector", lambda e, xb=xb: e.tensor_scalar(
                            xnt[:, :], xb[:, :], st["m"][:, :],
                            st["rstd"][:, :], ALU.subtract, ALU.mult),
                            (t_rstd, t_m) + p1s["xn_cons"])
                        p1s["cons"][i % 2] = t_xn
                        t_spxn = p.dma("gpsimd", xn_sp[r0:r0 + 128, :],
                                       xnt[:, :], (t_xn,), stream="xnt")
                        t_sq2 = p.op("scalar", lambda e: e.activation(
                            junkq[:, :D_MODEL], xnt[:, :], AF.Square,
                            accum_out=st["ssq2"][:, :]), (t_xn,))
                        t_l2 = p.op("scalar", lambda e: e.activation(
                            st["ssq2"][:, :], st["ssq2"][:, :], AF.Ln,
                            scale=1.0 / D_MODEL, bias=EPS), (t_sq2,))
                        t_rstd2 = p.op("scalar", lambda e: e.activation(
                            st["rstd2"][:, :], st["ssq2"][:, :], AF.Exp,
                            scale=-0.5), (t_l2,))
                        t_u = p.op("vector", lambda e: e.tensor_scalar(
                            ut[:, :], xnt[:, :], st["rstd2"][:, :], None,
                            ALU.mult), (t_rstd2, t_xn, T.get("ut_free")))
                        t_spu = p.dma("gpsimd", u_sp[r0:r0 + 128, :], ut[:, :],
                                      (t_u,), stream="ut")
                        T["ut_free"] = t_spu
                        T[f"u_done{b}"] = t_spu
                        p1s["xn_cons"] = (t_spxn, t_u)
                        qunits.append(_mk_qunit(
                            xn_sp[r0:r0 + 128, :], xq_sp[r0:r0 + 128, :],
                            D_MODEL, extra=t_spxn))

                    t_wT = []
                    wT_deps = (T.get("p0done"),)
                    for k in range(NK):
                        p.dmat(wT[:, k * WPAD:k * WPAD + 4352],
                               winp_bf[0:4352, k * 128:(k + 1) * 128],
                               deps=wT_deps if k == 0 else (),
                               stream="wT")
                        t_wT.append(p.dmat(
                            wT[:, k * WPAD + 4352:(k + 1) * WPAD],
                            winp_bf[4352:4384, k * 128:(k + 1) * 128],
                            stream="wT"))
                    t_wT = [t_wT[-1]] * NK

                    p1_pending = list(range(NB * NT))
                    for _ in range(NT):
                        emit_p1_tile(p1_pending.pop(0))

                    def feed():
                        if p1_pending:
                            emit_p1_tile(p1_pending.pop(0))
                        else:
                            emit_qunit()
                    for b in range(NB):
                        if b == 1:
                            while p1_pending:
                                emit_p1_tile(p1_pending.pop(0))
                        t_uT = []
                        for k in range(NK):
                            t_uT.append(p.dmat(
                                uT[:, k * L:(k + 1) * L],
                                u_sp[b * L:(b + 1) * L, k * 128:(k + 1) * 128],
                                (T.get("uT_free"), T.get(f"u_done{b}")),
                                stream="uT"))
                        t_uT = [t_uT[-1]] * NK
                        bankB = [T.get("psB")] * 4
                        for t in range(NT):
                            zevs = []
                            for j in range(4):
                                sl = psB[:, j * 512:(j + 1) * 512]
                                mts = []
                                for k in range(NK):
                                    mts.append(p.op("tensor",
                                        lambda e, sl=sl, k=k, t=t, j=j: e.matmul(
                                            sl,
                                            uT[:, k * L + t * 128:
                                               k * L + (t + 1) * 128],
                                            wT[:, k * WPAD + j * 512:
                                               k * WPAD + (j + 1) * 512],
                                            start=(k == 0), stop=(k == NK - 1)),
                                        (t_wT[k], t_uT[k], bankB[j])))
                                ev = p.op("scalar", lambda e, sl=sl, j=j:
                                          e.activation(
                                              zst[:, :], sl, AF.Silu),
                                          (mts[-1], T.get("zst_free")))
                                bankB[j] = ev
                                r0 = b * L + t * 128
                                t_zs = p.dma(
                                    "gpsimd",
                                    sz_sp[r0:r0 + 128, j * 512:(j + 1) * 512],
                                    zst[:, :], (ev,), stream="zst")
                                T["zst_free"] = t_zs
                                zevs.append(ev)
                            if b == 0:
                                feed()
                            else:
                                emit_qunit()
                        bankA = [T.get("psA")] * 4
                        for c in range(NC18):
                            evs = []
                            t_pad = p.op("gpsimd", lambda e: e.memset(
                                pre[:, 0:3], 0.0), (T.get("pre_free"),))
                            for j in range(4):
                                sl = psA[:, j * 512:(j + 1) * 512]
                                mts = []
                                for k in range(NK):
                                    fcol = D_INNER + c * 128
                                    mts.append(p.op("tensor",
                                        lambda e, sl=sl, k=k, c=c, j=j,
                                        fcol=fcol: e.matmul(
                                            sl,
                                            wT[:, k * WPAD + fcol:
                                               k * WPAD + fcol + 128],
                                            uT[:, k * L + j * 512:
                                               k * L + (j + 1) * 512],
                                            start=(k == 0), stop=(k == NK - 1)),
                                        (t_wT[k], t_uT[k], bankA[j])))
                                ev = p.op("vector", lambda e, sl=sl, j=j:
                                          e.tensor_copy(
                                              pre[:, 3 + j * 512:
                                                  3 + (j + 1) * 512], sl),
                                          (mts[-1],))
                                bankA[j] = ev
                                evs.append(ev)
                            p.nop("vector")
                            t_t0 = p.op("vector", lambda e, c=c: e.tensor_scalar(
                                accA[:, :], pre[:, 0:L],
                                convw[:, 4 * c:4 * c + 1], None, ALU.mult),
                                tuple(evs) + (t_pad, t_convw,
                                              T.get("accA_free")))
                            p.nop("vector")
                            t_t1 = p.op("vector", lambda e, c=c:
                                        e.scalar_tensor_tensor(
                                            accB[:, :], pre[:, 1:L + 1],
                                            convw[:, 4 * c + 1:4 * c + 2],
                                            accA[:, :], ALU.mult, ALU.add),
                                        (t_t0, T.get("accB_free")))
                            p.nop("vector")
                            t_t2 = p.op("vector", lambda e, c=c:
                                        e.scalar_tensor_tensor(
                                            accA[:, :], pre[:, 2:L + 2],
                                            convw[:, 4 * c + 2:4 * c + 3],
                                            accB[:, :], ALU.mult, ALU.add),
                                        (t_t1,))
                            p.nop("vector")
                            t_t3 = p.op("vector", lambda e, c=c:
                                        e.scalar_tensor_tensor(
                                            accB[:, :], pre[:, 3:L + 3],
                                            convw[:, 4 * c + 3:4 * c + 4],
                                            accA[:, :], ALU.mult, ALU.add),
                                        (t_t2,))
                            T["pre_free"] = t_t3
                            T["accA_free"] = t_t3
                            dst = (accA[:, :] if c < 16 else
                                   (bt_t[b][:, :] if c == 16 else ct_t[b][:, :]))
                            t_act = p.op("scalar", lambda e, c=c, dst=dst:
                                         e.activation(dst, accB[:, :], AF.Silu,
                                                      bias=convb[:, c:c + 1]),
                                         (t_t3, t_convb,
                                          T.get("xbcst_free") if c < 16
                                          else None))
                            T["accB_free"] = t_act
                            if c < 16:
                                T["xbcst_free"] = p.dma(
                                    "gpsimd", xs_sp[b, c * 128:(c + 1) * 128, :],
                                    accA[:, :], (t_act,), stream="xbcst")
                                T["accA_free"] = T["xbcst_free"]
                            if b == 0:
                                feed()
                            else:
                                emit_qunit()
                        T["psA"] = bankA[3]
                        for t in range(NT):
                            sl = psB[:, (t % 4) * 512:(t % 4) * 512 + NH]
                            mts = []
                            for k in range(NK):
                                mts.append(p.op("tensor",
                                    lambda e, sl=sl, k=k, t=t: e.matmul(
                                        sl,
                                        uT[:, k * L + t * 128:
                                           k * L + (t + 1) * 128],
                                        wT[:, k * WPAD + 4352:
                                           k * WPAD + 4384],
                                        start=(k == 0), stop=(k == NK - 1)),
                                    (t_wT[k], t_uT[k], bankB[t % 4])))
                            ev = p.op("vector", lambda e, sl=sl, b=b, t=t:
                                      e.tensor_copy(
                                          dt_sp[b][:, t * NH:(t + 1) * NH], sl),
                                      (mts[-1],))
                            bankB[t % 4] = ev
                            emit_qunit()
                        T["psB"] = bankB[3]
                        T["uT_free"] = bankB[3]
                p.barrier()

            if "p3" in phases:
                import contextlib as _cl
                with _cl.ExitStack() as ph:
                    g_ = alloc_group(ph, [
                        ("xs0", (128, D_INNER), BF16),
                        ("xs1", (128, D_INNER), BF16),
                        ("szb0", (128, D_INNER), BF16),
                        ("szb1", (128, D_INNER), BF16),
                        ("sp0", (128, D_INNER), F32),
                        ("sp1", (128, D_INNER), F32),
                        ("spb", (128, D_INNER), BF16),
                        ("rhs1", (128, 4096), BF16),
                        ("mp", (128, 4096), BF16),
                        ("e1t", (128, D_INNER), BF16),
                        ("rlt", (128, D_INNER), F32),
                        ("xdt", (128, D_INNER), BF16),
                        ("xdd", (128, D_INNER), BF16),
                        ("t2b", (128, D_INNER), BF16),
                        ("t1f", (128, D_INNER), F32),
                        ("y1f", (128, D_INNER), F32),
                        ("y2f", (128, D_INNER), F32),
                        ("ygf", (128, D_INNER), F32),
                        ("ybt", (128, D_INNER), BF16),
                        ("acs", (32, 128), F32), ("acsT", (128, NH), F32),
                        ("dsb", (32, 128), BF16), ("dsT", (128, NH), BF16),
                        ("eat", (128, NH), F32), ("erow", (32, 1), F32),
                        ("ermat", (32, 32), F32), ("edr", (128, NH), F32),
                        ("gmt", (128, 128), BF16), ("bct", (128, 128), BF16),
                        ("aft", (128, NH), F32), ("abt", (128, NH), BF16),
                        ("nab", (128, NH), BF16),
                        ("spg1", (128, NT * NH), F32),
                        ("spg2", (128, NT * NH), F32)])
                    xs0 = g_["xs0"]; xs1 = g_["xs1"]
                    szb0 = g_["szb0"]; szb1 = g_["szb1"]
                    sp0 = g_["sp0"]; sp1 = g_["sp1"]; spb = g_["spb"]
                    rhs1 = g_["rhs1"]; mp = g_["mp"]; e1t = g_["e1t"]
                    rlt = g_["rlt"]; xdt = g_["xdt"]; xdd = g_["xdd"]
                    t2b = g_["t2b"]; t1f = g_["t1f"]; y1f = g_["y1f"]
                    y2f = g_["y2f"]; ygf = g_["ygf"]; ybt = g_["ybt"]
                    acs = g_["acs"]; acsT = g_["acsT"]; dsb = g_["dsb"]
                    dsT = g_["dsT"]; eat = g_["eat"]; erow = g_["erow"]
                    ermat = g_["ermat"]; edr = g_["edr"]; gmt = g_["gmt"]
                    bct = g_["bct"]; aft = g_["aft"]; abt = g_["abt"]
                    nab = g_["nab"]; spg1 = g_["spg1"]; spg2 = g_["spg2"]
                    for b in range(NB):
                        dtb_b = dtb_rep.ap().unsqueeze(1).broadcast_to((128, NT, NH))
                        dtr3 = dt_sp[b].ap().rearrange("p (t h) -> p t h", h=NH)
                        t_d0 = p.op("vector", lambda e, dtr3=dtr3, dtb_b=dtb_b:
                                    e.tensor_tensor(dtr3, dtr3, dtb_b, ALU.add),
                                    (T["dtb_rep"],))
                        t_g1 = p.op("scalar", lambda e, b=b: e.activation(
                            spg1[:, :], dt_sp[b][:, :], AF.Abs), (t_d0,))
                        t_g4 = p.op("scalar", lambda e, b=b: e.activation(
                            spg2[:, :], dt_sp[b][:, :], AF.Relu), (t_d0,))
                        t_g2 = p.op("scalar", lambda e: e.activation(
                            spg1[:, :], spg1[:, :], AF.Exp, scale=-1.0), (t_g1,))
                        t_g3 = p.op("scalar", lambda e: e.activation(
                            spg1[:, :], spg1[:, :], AF.Ln, bias=1.0), (t_g2,))
                        t_dt = p.op("vector", lambda e, b=b: e.tensor_tensor(
                            dt_sp[b][:, :], spg1[:, :], spg2[:, :], ALU.add),
                            (t_g3, t_g4))
                        if dt_dbg is not None:
                            p.dma("gpsimd", dt_dbg[b], dt_sp[b][:, :], (t_dt,),
                                  stream="dbg")
                        sps = [sp0, sp1]
                        t_spi = p.op("gpsimd", lambda e: e.memset(sp0[:, :], 0.0),
                                     (T.get("sp_free"),))
                        T["sp_prev_tok"] = t_spi
                        T["yoff_done"] = None
                        xsb = [xs0, xs1]
                        szb = [szb0, szb1]
                        t_xsl = [None, None]
                        t_szl = [None, None]
                        cons_xs = T.setdefault("cons_xs", [(), ()])
                        cons_sz = T.setdefault("cons_sz", [None, None])
                        t_xsl[0] = p.dmat(xs0[:, :], xs_sp[b, :, 0:128],
                                          cons_xs[0], stream="xsl0")
                        t_szl[0] = p.dma("sync", szb0[:, :],
                                         sz_sp[b * L:b * L + 128, :],
                                         (cons_sz[0],), stream="szl0")
                        for c in range(NT):
                            i = c % 2
                            ni = (c + 1) % 2
                            if c + 1 < NT:
                                t_xsl[ni] = p.dmat(
                                    xsb[ni][:, :],
                                    xs_sp[b, :, (c + 1) * 128:(c + 2) * 128],
                                    cons_xs[ni], stream=f"xsl{ni}")
                                t_szl[ni] = p.dma(
                                    "sync", szb[ni][:, :],
                                    sz_sp[b * L + (c + 1) * 128:
                                          b * L + (c + 2) * 128, :],
                                    (cons_sz[ni],), stream=f"szl{ni}")
                            xs_c = xsb[i]
                            sz_c = szb[i]
                            spP = sps[c % 2]
                            spN = sps[(c + 1) % 2]
                            ccol = slice(c * 128, (c + 1) * 128)
                            dslice = dt_sp[b][:, c * NH:(c + 1) * NH]
                            t_af = p.op("vector", lambda e, dslice=dslice:
                                        e.tensor_tensor(aft[:, :], dslice,
                                                        a_rep[:, :], ALU.mult),
                                        (t_dt, T["a_rep"], T.get("af_free")))
                            p.nop("vector")
                            t_ab = p.op("vector", lambda e: e.tensor_copy(
                                abt[:, :], aft[:, :]),
                                (t_af, T.get("abt_r")))
                            t_nab = p.op("vector", lambda e: e.tensor_scalar(
                                nab[:, :], aft[:, :], -1.0, None, ALU.mult),
                                (t_af, T.get("nab_free")))
                            t_m1 = p.op("tensor", lambda e: e.matmul(
                                psB[0:32, 0:128], aft[:, :], u1f[:, :],
                                start=True, stop=True),
                                (t_af, t_u1, T.get("psB")))
                            t_m2 = p.op("tensor", lambda e: e.matmul(
                                psB[:, 128:128 + NH], u1f[:, :], aft[:, :],
                                start=True, stop=True), (t_af, t_u1))
                            t_acs = p.op("vector", lambda e: e.tensor_copy(
                                acs[:, :], psB[0:32, 0:128]),
                                (t_m1, t_m2, T.get("acs_free")))
                            t_acsT = p.op("vector", lambda e: e.tensor_copy(
                                acsT[:, :], psB[:, 128:128 + NH]),
                                (t_m2, T.get("acsT_free")))
                            T["af_free"] = t_m2
                            t_ds = p.op("scalar", lambda e: e.activation(
                                dsb[:, :], acs[:, :], AF.Exp, scale=-1.0,
                                bias=acs[:, 127:128]),
                                (t_acs, T.get("dsb_free")))
                            t_eat = p.op("scalar", lambda e: e.activation(
                                eat[:, :], acsT[:, :], AF.Exp),
                                (t_acsT, T.get("eat_free")))
                            t_er = p.op("scalar", lambda e: e.activation(
                                erow[:, :], acs[:, 127:128], AF.Exp),
                                (t_acs, T.get("erow_free")))
                            t_em0 = p.op("gpsimd", lambda e: e.memset(
                                ermat[:, :], 0.0), (T.get("ermat_free"),))
                            erb = erow.ap().broadcast_to((32, 32))
                            t_em = p.op("gpsimd", lambda e, erb=erb:
                                        e.affine_select(
                                            out=ermat[:, :], in_=erb,
                                            compare_op=ALU.is_equal, fill=0.0,
                                            base=0, pattern=[[-1, 32]],
                                            channel_multiplier=1),
                                        (t_er, t_em0))
                            T["erow_free"] = t_em
                            t_edm = p.op("tensor", lambda e: e.matmul(
                                psB[:, 160:160 + NH], ones_f[0:32, :],
                                ermat[:, :], start=True, stop=True),
                                (t_em, t_onesf, t_acsT))
                            T["ermat_free"] = t_edm
                            t_edr = p.op("vector", lambda e: e.tensor_copy(
                                edr[:, :], psB[:, 160:160 + NH]), (t_edm,))
                            t_gmm = p.op("tensor", lambda e, b=b, ccol=ccol:
                                         e.matmul(psB[:, 256:384],
                                                  bt_t[b][:, ccol],
                                                  ct_t[b][:, ccol],
                                                  start=True, stop=True),
                                         (t_edr,))
                            t_gev = p.op("vector", lambda e: e.tensor_copy(
                                gmt[:, :], psB[:, 256:384]),
                                (t_gmm, T.get("gmask_free")))
                            t_gmask = p.op("gpsimd", lambda e: e.affine_select(
                                out=gmt[:, :], in_=gmt[:, :],
                                compare_op=ALU.is_ge, fill=0.0, base=0,
                                pattern=[[1, 128]], channel_multiplier=-1),
                                (t_gev,))
                            T["gmask_free"] = t_gmask
                            t_dst_mm = p.op("tensor", lambda e: e.transpose(
                                psBb[:, 384:384 + NH], dsb[:, :],
                                ident[0:32, 0:32]), (t_ds, t_id, t_gev))
                            T["dsb_free"] = t_dst_mm
                            t_dsT = p.op("vector", lambda e: e.tensor_copy(
                                dsT[:, :], psBb[:, 384:384 + NH]), (t_dst_mm,))
                            ab3 = abt.ap().unsqueeze(2).broadcast_to(
                                (128, NH, 128))
                            u13 = u1b.ap().unsqueeze(1).broadcast_to(
                                (128, NH, 128))
                            r13 = rhs1.ap().rearrange("p (h l) -> p h l", l=128)
                            t_r1 = p.op("gpsimd", lambda e, ab3=ab3, u13=u13,
                                        r13=r13: e.tensor_tensor(
                                            r13, ab3, u13, ALU.mult),
                                        (t_ab, t_u1b, T.get("rhs1_free")))
                            T["abt_r"] = t_r1
                            t_mp = [None, None]
                            t_relu_prev = T.get("psA")
                            t_mm2_last = None
                            t_mp_prevgrp = T.get("mp_free")
                            t_e1_prev = T.get("e1_free")
                            for g in range(2):
                                mts = []
                                for j in range(4):
                                    mts.append(p.op("tensor",
                                        lambda e, g=g, j=j: e.matmul(
                                            psA[:, j * 512:(j + 1) * 512],
                                            ones_b[:, :],
                                            rhs1[:, g * 2048 + j * 512:
                                                 g * 2048 + (j + 1) * 512],
                                            start=True, stop=False),
                                        (t_r1, t_onesb, t_relu_prev)))
                                for j in range(4):
                                    h0 = g * 16 + j * 4
                                    nb3 = nab[:, h0:h0 + 4].unsqueeze(2).broadcast_to(
                                            (128, 4, 128))
                                    t_mm2_last = p.op("tensor",
                                        lambda e, j=j, nb3=nb3: e.matmul(
                                            psA[:, j * 512:(j + 1) * 512],
                                            u1b[:, :], nb3,
                                            start=False, stop=True),
                                        (t_nab, t_u1b))
                                    mts.append(t_mm2_last)
                                t_rl = p.op("scalar", lambda e: e.activation(
                                    rlt[:, :], psA[:, :], AF.Relu, scale=-1.0),
                                    tuple(mts) + (T.get("rlt_r"),))
                                t_e1 = p.op("scalar", lambda e: e.activation(
                                    e1t[:, :], rlt[:, :], AF.Exp, scale=-1.0),
                                    (t_rl, t_mp_prevgrp, t_e1_prev))
                                T["rlt_r"] = t_e1
                                t_e1_prev = t_e1
                                gm3 = gmt.ap().unsqueeze(1).broadcast_to((128, 16, 128))
                                e13 = e1t.ap().rearrange(
                                    "p (h l) -> p h l", l=128)
                                mp3 = mp[:, g * 2048:(g + 1) * 2048].rearrange(
                                    "p (h l) -> p h l", l=128)
                                t_mp[g] = p.op("gpsimd",
                                    lambda e, e13=e13, mp3=mp3, gm3=gm3:
                                    e.tensor_tensor(mp3, e13, gm3, ALU.mult),
                                    (t_e1, t_gmask, T.get("mp_free")))
                                t_mp_prevgrp = t_mp[g]
                                t_relu_prev = t_rl
                            T["rhs1_free"] = t_relu_prev
                            T["nab_free"] = t_mm2_last
                            T["e1_free"] = t_mp[1]
                            T["gmask_free"] = t_mp[1]
                            dt3 = dslice.unsqueeze(2).broadcast_to(
                                (128, NH, HD))
                            xs3 = xs_c.ap().rearrange("p (h d) -> p h d", d=HD)
                            xd3 = xdt.ap().rearrange("p (h d) -> p h d", d=HD)
                            t_xd = p.op("vector", lambda e, xs3=xs3, xd3=xd3,
                                        dt3=dt3: e.tensor_tensor(
                                            xd3, xs3, dt3, ALU.mult),
                                        (t_xsl[i], t_dt, T.get("xd_free")))
                            ds3 = dsT.ap().unsqueeze(2).broadcast_to(
                                (128, NH, HD))
                            xdd3 = xdd.ap().rearrange("p (h d) -> p h d", d=HD)
                            p.nop("vector")
                            t_xdd = p.op("vector", lambda e, xd3=xd3, xdd3=xdd3,
                                         ds3=ds3: e.tensor_tensor(
                                             xdd3, xd3, ds3, ALU.mult),
                                         (t_xd, t_dsT, T.get("xdd_free")))
                            d3 = d_rep.ap().unsqueeze(2).broadcast_to(
                                (128, NH, HD))
                            t2_3 = t2b.ap().rearrange("p (h d) -> p h d", d=HD)
                            t_t2 = p.op("gpsimd", lambda e, xs3=xs3, t2_3=t2_3,
                                        d3=d3: e.tensor_tensor(
                                            t2_3, xs3, d3, ALU.mult),
                                        (t_xsl[i], T["d_rep"],
                                         T.get("t2_free")))
                            cons_xs[i] = (t_t2, t_xd)
                            t_bt_mm = p.op("tensor", lambda e, b=b, ccol=ccol:
                                           e.transpose(psBb[:, 512:640],
                                                       bt_t[b][:, ccol],
                                                       ident[:, :]),
                                           (t_id, t_dsT))
                            t_bc = p.op("vector", lambda e: e.tensor_copy(
                                bct[:, :], psBb[:, 512:640]),
                                (t_bt_mm, T.get("bc_free")))
                            t_spb = p.op("vector", lambda e, spP=spP:
                                         e.tensor_copy(spb[:, :], spP[:, :]),
                                         (T["sp_prev_tok"],
                                          T.get("spb_free")))
                            smts = []
                            for j in range(4):
                                smts.append(p.op("tensor",
                                    lambda e, j=j: e.matmul(
                                        psA[:, j * 512:(j + 1) * 512], bct[:, :],
                                        xdd[:, j * 512:(j + 1) * 512],
                                        start=True, stop=True),
                                    (t_bc, t_xdd, t_relu_prev)))
                            T["bc_free"] = smts[-1]
                            ed3 = edr.ap().unsqueeze(2).broadcast_to(
                                (128, NH, HD))
                            spP3 = spP.ap().rearrange("p (h d) -> p h d", d=HD)
                            spN3 = spN.ap().rearrange("p (h d) -> p h d", d=HD)
                            t_su1 = p.op("vector", lambda e, spN3=spN3,
                                         spP3=spP3, ed3=ed3: e.tensor_tensor(
                                             spN3, spP3, ed3, ALU.mult),
                                         (T["sp_prev_tok"], t_edr,
                                          T.get("yoff_done")))
                            p.nop("vector")
                            t_su2 = p.op("vector", lambda e, spN=spN:
                                         e.tensor_tensor(
                                             spN[:, :], spN[:, :], psA[:, :],
                                             ALU.add), (t_su1, smts[-1]))
                            ymts = []
                            for h in range(NH):
                                ymts.append(p.op("tensor",
                                    lambda e, h=h: e.matmul(
                                        psB[:, h * HD:(h + 1) * HD],
                                        mp[:, h * 128:(h + 1) * 128],
                                        xdt[:, h * HD:(h + 1) * HD],
                                        start=(h % 8 == 0), stop=(h % 8 == 7)),
                                    (t_mp[h // 16], t_xd, t_bc, t_edr,
                                     t_gev, t_dsT, T.get("y_free"))))
                            T["mp_free"] = ymts[-1]
                            T["xd_free"] = ymts[-1]
                            omts = []
                            for j in range(4):
                                omts.append(p.op("tensor",
                                    lambda e, j=j, b=b, ccol=ccol: e.matmul(
                                        psA[:, j * 512:(j + 1) * 512],
                                        ct_t[b][:, ccol],
                                        spb[:, j * 512:(j + 1) * 512],
                                        start=True, stop=True),
                                    (t_spb, t_su2)))
                            T["sp_prev_tok"] = t_su2
                            T["yoff_done"] = omts[-1]
                            T["spb_free"] = omts[-1]
                            T["xdd_free"] = t_su2
                            eat3 = eat.ap().unsqueeze(2).broadcast_to(
                                (128, NH, HD))
                            psA3 = psA.ap().rearrange("p (h d) -> p h d", d=HD)
                            t1_3 = t1f.ap().rearrange("p (h d) -> p h d", d=HD)
                            t_t1f = p.op("vector", lambda e, t1_3=t1_3,
                                         psA3=psA3, eat3=eat3: e.tensor_tensor(
                                             t1_3, psA3, eat3, ALU.mult),
                                         (omts[-1], t_eat))
                            T["psA"] = t_t1f
                            T["eat_free"] = t_t1f
                            p.nop("vector")
                            t_y1 = p.op("vector", lambda e: e.tensor_tensor(
                                y1f[:, :], psB[:, :], t1f[:, :], ALU.add),
                                (ymts[-1], t_t1f))
                            T["psB"] = t_y1
                            T["y_free"] = t_y1
                            T["acs_free"] = t_y1
                            T["acsT_free"] = t_y1
                            p.nop("vector")
                            t_y2 = p.op("vector", lambda e: e.tensor_tensor(
                                y2f[:, :], y1f[:, :], t2b[:, :], ALU.add),
                                (t_y1, t_t2))
                            T["t2_free"] = t_y2
                            p.nop("vector")
                            t_yg = p.op("vector", lambda e, sz_c=sz_c:
                                        e.tensor_tensor(ygf[:, :], y2f[:, :],
                                                        sz_c[:, :], ALU.mult),
                                        (t_y2, t_szl[i]))
                            cons_sz[i] = t_yg
                            t_sq3 = p.op("scalar", lambda e: e.activation(
                                rlt[:, :], ygf[:, :], AF.Square,
                                accum_out=st["ssq3"][:, :]),
                                (t_yg, T.get("rlt_r")))
                            T["rlt_r"] = t_sq3
                            t_l3 = p.op("scalar", lambda e: e.activation(
                                st["l3"][:, :], st["ssq3"][:, :], AF.Ln,
                                scale=1.0 / D_INNER, bias=EPS), (t_sq3,))
                            t_r3 = p.op("scalar", lambda e: e.activation(
                                st["rstd3"][:, :], st["l3"][:, :], AF.Exp,
                                scale=-0.5), (t_l3,))
                            p.nop("vector")
                            t_yb = p.op("vector", lambda e: e.tensor_scalar(
                                ybt[:, :], ygf[:, :], st["rstd3"][:, :], None,
                                ALU.mult), (t_r3, t_yg, T.get("ybt_free")))
                            r0 = b * L + c * 128
                            T["ybt_free"] = p.dma(
                                "gpsimd", yb_sp[r0:r0 + 128, :], ybt[:, :],
                                (t_yb,), stream="ybt")
                            emit_qunit()
                        T["sp_free"] = T["sp_prev_tok"]
                    while qunits:
                        emit_qunit()
                p.barrier()
                qstk.close()

            if "p3b" in phases:
                import contextlib as _cl
                with _cl.ExitStack() as ph:
                    g_ = alloc_group(ph, [
                        ("woT", (128, 16 * D_MODEL), BF16),
                        ("ybT0", (128, 16 * 512), BF16),
                        ("ybT1", (128, 16 * 512), BF16),
                        ("xng0", (128, 4 * D_MODEL), F32),
                        ("xng1", (128, 4 * D_MODEL), F32),
                        ("sstg", (128, D_MODEL), F32)])
                    woT = g_["woT"]; ybTs = [g_["ybT0"], g_["ybT1"]]
                    xngs = [g_["xng0"], g_["xng1"]]; sstg = g_["sstg"]
                    t_wo = []
                    for k in range(16):
                        t_wo.append(p.dmat(woT[:, k * D_MODEL:(k + 1) * D_MODEL],
                                           wout_bf[:, k * 128:(k + 1) * 128],
                                           stream="woT"))
                    t_wo = [t_wo[-1]] * 16
                    for b in range(NB):
                        for g in range(4):
                            gi = (b * 4 + g) % 2
                            ybT = ybTs[gi]
                            xng = xngs[gi]
                            r0 = b * L + g * 512
                            t_yl = []
                            for k in range(16):
                                t_yl.append(p.dmat(
                                    ybT[:, k * 512:(k + 1) * 512],
                                    yb_sp[r0:r0 + 512, k * 128:(k + 1) * 128],
                                    (T.get(f"ybT_free{gi}"),),
                                    stream=f"ybT{gi}"))
                            t_yl = [t_yl[-1]] * 16
                            t_xl = []
                            for j4 in range(4):
                                t_xl.append(p.dma(
                                    "sync",
                                    xng[:, j4 * D_MODEL:(j4 + 1) * D_MODEL],
                                    xn_sp[r0 + j4 * 128:r0 + (j4 + 1) * 128, :],
                                    (T.get(f"ybT_free{gi}"),),
                                    stream=f"xng{gi}"))
                            t_xl = [t_xl[-1]] * 4
                            t_s_last = None
                            for j4 in range(4):
                                ps = (psA if j4 < 2 else psB)
                                off = (j4 % 2) * 1024
                                psfree = T.get("psA") if j4 < 2 else T.get("psB")
                                mts = []
                                for k in range(16):
                                    for n2 in range(2):
                                        mts.append(p.op("tensor",
                                            lambda e, ps=ps, off=off, k=k,
                                            j4=j4, n2=n2, ybT=ybT: e.matmul(
                                                ps[:, off + n2 * 512:
                                                   off + (n2 + 1) * 512],
                                                ybT[:, k * 512 + j4 * 128:
                                                    k * 512 + (j4 + 1) * 128],
                                                woT[:, k * D_MODEL + n2 * 512:
                                                    k * D_MODEL +
                                                    (n2 + 1) * 512],
                                                start=(k == 0), stop=(k == 15)),
                                            (t_yl[k], t_wo[k], psfree)))
                                t_s = p.op("vector", lambda e, ps=ps, off=off,
                                           j4=j4, xng=xng: e.tensor_tensor(
                                               sstg[:, :],
                                               ps[:, off:off + 1024],
                                               xng[:, j4 * D_MODEL:
                                                   (j4 + 1) * D_MODEL],
                                               ALU.add),
                                           (mts[-1], t_xl[j4],
                                            T.get("sstg_free")))
                                if j4 == 1:
                                    T["psA"] = t_s
                                if j4 == 3:
                                    T["psB"] = t_s
                                t_s_last = t_s
                                T["sstg_free"] = p.dma(
                                    "gpsimd",
                                    s_sp[r0 + j4 * 128:r0 + (j4 + 1) * 128, :],
                                    sstg[:, :], (t_s,), stream="sstg")
                            T[f"ybT_free{gi}"] = t_s_last
                p.barrier()

            if "p4a" in phases:
                import contextlib as _cl
                with _cl.ExitStack() as ph:
                    g_ = alloc_group(ph, [
                        ("xqT", (128, NK * L), BF16),
                        ("qgT", (128, NK * DFF), BF16),
                        ("quT", (128, NK * DFF), BF16),
                        ("hch0", (128, 960), F32), ("sgch0", (128, 960), BF16),
                        ("hch1", (128, 960), F32), ("sgch1", (128, 960), BF16),
                        ("hqt", (128, DFF), BF16)])
                    xqT = g_["xqT"]; qgT = g_["qgT"]; quT = g_["quT"]
                    hchs = [g_["hch0"], g_["hch1"]]
                    sgchs = [g_["sgch0"], g_["sgch1"]]
                    hqt = g_["hqt"]
                    q4sl = [alloc_qscr(ph, "q4A", 960),
                            alloc_qscr(ph, "q4B", 960)]
                    q4i = {"i": 0}
                    t_qg = []
                    t_qu = []
                    for k in range(NK):
                        t_qg.append(p.dmat(qgT[:, k * DFF:(k + 1) * DFF],
                                           qg_bf[:, k * 128:(k + 1) * 128],
                                           stream="qgu"))
                        t_qu.append(p.dmat(quT[:, k * DFF:(k + 1) * DFF],
                                           qu_bf[:, k * 128:(k + 1) * 128],
                                           stream="qgu"))
                    t_qg = [t_qu[-1]] * NK
                    t_qu = [t_qu[-1]] * NK
                    chunks = [(0, 960), (960, 960), (1920, DFF - 1920)]
                    for b in range(NB):
                        t_xq = []
                        for k in range(NK):
                            t_xq.append(p.dmat(
                                xqT[:, k * L:(k + 1) * L],
                                xq_sp[b * L:(b + 1) * L, k * 128:(k + 1) * 128],
                                (T.get("xqT_free"),), stream="xqT"))
                        t_xq = [t_xq[-1]] * NK
                        for t in range(NT):
                            r0 = b * L + t * 128
                            qtoks = []
                            for ci, (f0, fw) in enumerate(chunks):
                                pi = q4i["i"] % 2
                                sl2 = q4i["i"] * 2
                                q4i["i"] += 1
                                gslot = sl2 % 4
                                uslot = (sl2 + 1) % 4
                                pg = (psA if gslot < 2 else psB)[
                                    :, (gslot % 2) * 1024:
                                       (gslot % 2) * 1024 + fw]
                                pu = (psA if uslot < 2 else psB)[
                                    :, (uslot % 2) * 1024:
                                       (uslot % 2) * 1024 + fw]
                                depg = T.get(f"p4g{gslot}",
                                             T.get("psA") if gslot < 2
                                             else T.get("psB"))
                                depu = T.get(f"p4g{uslot}",
                                             T.get("psA") if uslot < 2
                                             else T.get("psB"))
                                gmts, umts = [], []
                                for h0 in range(0, fw, 512):
                                    mw = min(512, fw - h0)
                                    for k in range(NK):
                                        gmts.append(p.op("tensor",
                                            lambda e, pg=pg, k=k, t=t, f0=f0,
                                            h0=h0, mw=mw: e.matmul(
                                                pg[:, h0:h0 + mw],
                                                xqT[:, k * L + t * 128:
                                                    k * L + (t + 1) * 128],
                                                qgT[:, k * DFF + f0 + h0:
                                                    k * DFF + f0 + h0 + mw],
                                                start=(k == 0),
                                                stop=(k == NK - 1)),
                                            (t_xq[k], t_qg[k], depg)))
                                for h0 in range(0, fw, 512):
                                    mw = min(512, fw - h0)
                                    for k in range(NK):
                                        umts.append(p.op("tensor",
                                            lambda e, pu=pu, k=k, t=t, f0=f0,
                                            h0=h0, mw=mw: e.matmul(
                                                pu[:, h0:h0 + mw],
                                                xqT[:, k * L + t * 128:
                                                    k * L + (t + 1) * 128],
                                                quT[:, k * DFF + f0 + h0:
                                                    k * DFF + f0 + h0 + mw],
                                                start=(k == 0),
                                                stop=(k == NK - 1)),
                                            (t_xq[k], t_qu[k], depu)))
                                hch = hchs[pi]
                                sgch = sgchs[pi]
                                t_sg = p.op("scalar", lambda e, pg=pg, fw=fw,
                                            sgch=sgch:
                                            e.activation(sgch[:, :fw], pg,
                                                         AF.Silu),
                                            (gmts[-1],
                                             T.get(f"sgch_free{pi}")))
                                t_h = p.op("vector", lambda e, pu=pu, fw=fw,
                                           hch=hch, sgch=sgch:
                                           e.tensor_tensor(
                                               hch[:, :fw], sgch[:, :fw], pu,
                                               ALU.mult),
                                           (t_sg, umts[-1],
                                            T.get(f"hch_free{pi}")))
                                t_q = quant(hch[:, :fw], hqt[:, f0:f0 + fw],
                                            fw, q4sl[pi](fw),
                                            (t_h, T.get("hqt_free")),
                                            chain=f"h{pi}")
                                qtoks.append(t_q)
                                T[f"p4g{gslot}"] = t_sg
                                T[f"p4g{uslot}"] = t_h
                                T[f"sgch_free{pi}"] = t_h
                                T[f"hch_free{pi}"] = t_q
                            T["hqt_free"] = p.dma(
                                "gpsimd", hq_sp[r0:r0 + 128, :], hqt[:, :],
                                tuple(qtoks), stream="hqt")
                        T["xqT_free"] = T["hqt_free"]
                p.barrier()

            if "p4b" in phases:
                import contextlib as _cl
                with _cl.ExitStack() as ph:
                    g_ = alloc_group(ph, [
                        ("qdT", (128, 22 * D_MODEL), BF16),
                        ("hqT0", (128, 22 * 512), BF16),
                        ("hqT1", (128, 22 * 512), BF16),
                        ("sg40", (128, 4 * D_MODEL), F32),
                        ("sg41", (128, 4 * D_MODEL), F32),
                        ("xg40", (128, 4 * D_MODEL), F32),
                        ("xg41", (128, 4 * D_MODEL), F32),
                        ("gti", (128, D_MODEL), F32),
                        ("fstg", (128, D_MODEL), F32)])
                    qdT = g_["qdT"]; hqTs = [g_["hqT0"], g_["hqT1"]]
                    sg4s = [g_["sg40"], g_["sg41"]]
                    xg4s = [g_["xg40"], g_["xg41"]]
                    gti = g_["gti"]; fstg = g_["fstg"]
                    t_qd = []
                    for k in range(22):
                        t_qd.append(p.dmat(qdT[:, k * D_MODEL:(k + 1) * D_MODEL],
                                           qd_bf[:, k * 128:(k + 1) * 128],
                                           stream="qdT"))
                    t_qd = [t_qd[-1]] * 22
                    for b in range(NB):
                        for g in range(4):
                            gi = (b * 4 + g) % 2
                            hqT = hqTs[gi]
                            sg4 = sg4s[gi]
                            xg4 = xg4s[gi]
                            r0 = b * L + g * 512
                            t_hl = []
                            for k in range(22):
                                t_hl.append(p.dmat(
                                    hqT[:, k * 512:(k + 1) * 512],
                                    hq_sp[r0:r0 + 512, k * 128:(k + 1) * 128],
                                    (T.get(f"hqT_free{gi}"),),
                                    stream=f"hqT{gi}"))
                            t_hl = [t_hl[-1]] * 22
                            t_sl = []
                            t_xl = []
                            for j4 in range(4):
                                t_sl.append(p.dma(
                                    "sync",
                                    sg4[:, j4 * D_MODEL:(j4 + 1) * D_MODEL],
                                    s_sp[r0 + j4 * 128:r0 + (j4 + 1) * 128, :],
                                    (T.get(f"hqT_free{gi}"),),
                                    stream=f"sgx{gi}"))
                                t_xl.append(p.dma(
                                    "sync",
                                    xg4[:, j4 * D_MODEL:(j4 + 1) * D_MODEL],
                                    x_in[b, g * 512 + j4 * 128:
                                         g * 512 + (j4 + 1) * 128, :],
                                    (T.get(f"hqT_free{gi}"),),
                                    stream=f"sgx{gi}"))
                            t_sl = [t_xl[-1]] * 4
                            t_xl = [t_xl[-1]] * 4
                            t_g_sc = [None, None]
                            for j4 in range(4):
                                ps = (psA if j4 < 2 else psB)
                                off = (j4 % 2) * 1024
                                psfree = T.get("psA") if j4 < 2 else T.get("psB")
                                mts = []
                                for k in range(22):
                                    for n2 in range(2):
                                        mts.append(p.op("tensor",
                                            lambda e, ps=ps, off=off, k=k,
                                            n2=n2, j4=j4, hqT=hqT: e.matmul(
                                                ps[:, off + n2 * 512:
                                                   off + (n2 + 1) * 512],
                                                hqT[:, k * 512 + j4 * 128:
                                                    k * 512 + (j4 + 1) * 128],
                                                qdT[:, k * D_MODEL + n2 * 512:
                                                    k * D_MODEL +
                                                    (n2 + 1) * 512],
                                                start=(k == 0),
                                                stop=(k == 21)),
                                            (t_hl[k], t_qd[k],
                                             psfree if k == 0 else None)))
                                t_g = p.op("scalar", lambda e, ps=ps, off=off:
                                           e.activation(
                                               gti[:, :], ps[:, off:off + 1024],
                                               AF.Silu),
                                           (mts[-1], T.get("gti_free")))
                                if j4 == 1:
                                    T["psA"] = t_g
                                if j4 == 3:
                                    T["psB"] = t_g
                                if g_dbg is not None:
                                    p.dma("gpsimd",
                                          g_dbg[r0 + j4 * 128:
                                                r0 + (j4 + 1) * 128, :],
                                          gti[:, :], (t_g,), stream="dbg")
                                t_f1 = p.op("vector", lambda e, j4=j4, sg4=sg4:
                                            e.tensor_tensor(
                                                fstg[:, :],
                                                sg4[:, j4 * D_MODEL:
                                                    (j4 + 1) * D_MODEL],
                                                gti[:, :], ALU.mult),
                                            (t_g, t_sl[j4],
                                             T.get("fstg_free")))
                                T["gti_free"] = t_f1
                                p.nop("vector")
                                t_f2 = p.op("vector", lambda e, j4=j4, xg4=xg4:
                                            e.tensor_tensor(
                                                fstg[:, :], fstg[:, :],
                                                xg4[:, j4 * D_MODEL:
                                                    (j4 + 1) * D_MODEL],
                                                ALU.add), (t_f1, t_xl[j4]))
                                T["fstg_free"] = p.dma(
                                    "gpsimd",
                                    out[b, g * 512 + j4 * 128:
                                        g * 512 + (j4 + 1) * 128, :],
                                    fstg[:, :], (t_f2,), stream="fstg")
                            T[f"hqT_free{gi}"] = T["fstg_free"]
                p.barrier()

        p.finalize(block)
    return nc


def _make_in_maps(inputs):
    x = np.ascontiguousarray(inputs["L0_attn_x"])
    common = {
        "norm_w": np.ascontiguousarray(inputs["L0_attn_norm_w"]),
        "in_proj_w": np.ascontiguousarray(inputs["L0_attn_in_proj_w"]),
        "conv_w": np.ascontiguousarray(inputs["L0_attn_conv_w"]),
        "conv_b": np.ascontiguousarray(inputs["L0_attn_conv_b"]),
        "dt_bias": np.ascontiguousarray(inputs["L0_attn_dt_bias"]),
        "A_log": np.ascontiguousarray(inputs["L0_attn_A_log"]),
        "D": np.ascontiguousarray(inputs["L0_attn_D"]),
        "gnorm_w": np.ascontiguousarray(inputs["L0_attn_gnorm_w"]),
        "out_proj_w": np.ascontiguousarray(inputs["L0_attn_out_proj_w"]),
        "ffn_w_gate": np.ascontiguousarray(inputs["L0_ffn_w_gate"]),
        "ffn_w_up": np.ascontiguousarray(inputs["L0_ffn_w_up"]),
        "ffn_w_down": np.ascontiguousarray(inputs["L0_ffn_w_down"]),
    }
    in_maps = []
    for i in range(8):
        m = dict(common)
        m["x"] = np.ascontiguousarray(x[2 * i:2 * i + 2])
        in_maps.append(m)
    return in_maps


def kernel(**inputs):
    nc = bass.Bass("TRN2", target_bir_lowering=False)
    build(nc)
    in_maps = _make_in_maps(inputs)
    res = run_bass_kernel_spmd(nc, in_maps, core_ids=list(range(8)))
    return np.concatenate([r["out"] for r in res.results],
                          axis=0).astype(np.float32)
